# version 15 (final)
# baseline (speedup 1.0000x reference)
import os
import contextlib
import numpy as np
import ml_dtypes
import concourse.bass as bass
import concourse.mybir as mybir
from concourse.bass_utils import run_bass_kernel_spmd

F32 = mybir.dt.float32
BF16 = mybir.dt.bfloat16
I8 = mybir.dt.int8
U8 = mybir.dt.uint8
AF = mybir.ActivationFunctionType
ALU = mybir.AluOpType
AX = mybir.AxisListType

T = 2048
NTC = 16
H = 1024
NHC = 8
II = 2816
NIC = 22
EPS = 1e-5
QMAX = 127.0
SQH = float(np.float32(np.sqrt(0.5)))

PH = "123"
DBG = ""


class Ctr:
    _i = 0

    def __init__(self):
        Ctr._i += 1
        self.name = f"c{Ctr._i}"
        self.h = None
        self.n = 0


def build_nc():
    nc = bass.Bass(trn_type="TRN2", target_bir_lowering=False)
    q = {k: [] for k in ("sync", "scalar", "vector", "tensor", "gpsimd")}

    d_x = nc.dram_tensor("x", [T, H], F32, kind="ExternalInput").ap()
    d_gamma = nc.dram_tensor("gamma", [H], F32, kind="ExternalInput").ap()
    d_wqp = nc.dram_tensor("wqp", [H, 256], F32, kind="ExternalInput").ap()
    d_wkv = nc.dram_tensor("wkv", [H, 256], F32, kind="ExternalInput").ap()
    d_woutp = nc.dram_tensor("woutp", [256, H], F32, kind="ExternalInput").ap()
    d_clipx = nc.dram_tensor("clipx", [1, 1], F32, kind="ExternalInput").ap()
    d_clipd = nc.dram_tensor("clipd", [1, 1], F32, kind="ExternalInput").ap()
    d_wg = nc.dram_tensor("wg", [II, H], F32, kind="ExternalInput").ap()
    d_wu = nc.dram_tensor("wu", [II, H], F32, kind="ExternalInput").ap()
    d_wd = nc.dram_tensor("wd", [H, II], F32, kind="ExternalInput").ap()
    d_idf = nc.dram_tensor("idf", [128, 128], F32, kind="ExternalInput").ap()
    d_ids = nc.dram_tensor("ids", [128, 128], F32, kind="ExternalInput").ap()
    d_out = nc.dram_tensor("out", [T, H], F32, kind="ExternalOutput").ap()

    d_wgT = nc.dram_tensor("wgT_s", [II, H], BF16, kind="Internal").ap()
    d_wuT = nc.dram_tensor("wuT_s", [II, H], BF16, kind="Internal").ap()
    d_wdT = nc.dram_tensor("wdT_s", [II, H], BF16, kind="Internal").ap()
    d_sg = nc.dram_tensor("sg_s", [II, T], F32, kind="Internal").ap()
    d_ccin = [nc.dram_tensor(f"ccin{i}", [1, 128], F32, kind="Internal").ap() for i in range(4)]
    d_ccout = [nc.dram_tensor(f"ccout{i}", [8, 128], F32, kind="Internal").ap() for i in range(4)]

    dumps = {}

    def dump_decl(name, shape):
        dumps[name] = nc.dram_tensor("dbg_" + name, shape, F32, kind="ExternalOutput").ap()
        return dumps[name]

    A = nc.alloc_sbuf_tensor
    s_x = A("s_x", [128, NTC * H], F32).ap()
    h_xiT = A("s_xiT", [128, NHC * T], BF16)
    s_xiT = h_xiT.ap()
    s_y = A("s_y", [128, 2 * H], F32).ap()
    s_q8 = A("s_q8", [128, 2 * H], I8).ap()
    s_qb = A("s_qb", [128, 2 * H], BF16).ap()
    s_idf = A("s_idf", [128, 128], F32).ap()
    s_idb = A("s_idb", [128, 128], BF16).ap()
    s_ids = A("s_ids", [128, 128], F32).ap()
    s_gam = A("s_gam", [128, 8], F32).ap()
    s_swg = A("s_swg", [128, 24], F32).ap()
    s_swu = A("s_swu", [128, 24], F32).ap()
    s_swd = A("s_swd", [128, 8], F32).ap()
    s_swdb = A("s_swdb", [128, H], F32).ap()
    s_swdb2 = A("s_swdb2", [128, H], F32).ap()
    s_stats = A("s_stats", [128, 192], F32).ap()
    s_msq = A("s_msq", [128, 64], F32).ap()
    s_m16 = A("s_m16", [128, 16], F32).ap()
    s_v16 = A("s_v16", [128, 16], F32).ap()
    s_r16 = A("s_r16", [128, 16], F32).ap()
    s_a16 = A("s_a16", [128, 16], F32).ap()
    s_b16 = A("s_b16", [128, 16], F32).ap()
    s_t16a = A("s_t16a", [128, 16], F32).ap()
    s_t16b = A("s_t16b", [128, 16], F32).ap()
    s_t16c = A("s_t16c", [128, 16], F32).ap()
    s_ymax = A("s_ymax", [128, 16], F32).ap()
    s_sgmax = A("s_sgmax", [128, 24], F32).ap()
    s_red1 = A("s_red1", [128, 2], F32).ap()
    s_row128 = A("s_row128", [1, 128], F32).ap()
    s_ones = A("s_ones", [128, 128], F32).ap()
    s_sel4 = A("s_sel4", [1, 512], F32).ap()
    s_g8 = A("s_g8", [8, 128], F32).ap()
    s_red8 = A("s_red8", [8, 2], F32).ap()
    s_sc = A("s_sc", [1, 16], F32).ap()
    s_sig = A("s_sig", [1, 4], F32).ap()
    s_col = A("s_col", [128, 8], F32).ap()
    s_deqg = A("s_deqg", [128, 24], F32).ap()
    s_dequ = A("s_dequ", [128, 24], F32).ap()
    s_rden = A("s_rden", [128, 128], F32).ap()
    s_cst = A("s_cst", [128, 4], F32).ap()
    POOL = 83968
    h_pool = A("s_pool", [128, POOL], U8)
    pool_addr = nc.lookup_mloc(h_pool).addr
    xi_addr = nc.lookup_mloc(h_xiT).addr
    _n = [0]

    def _at(base, off, shape, dtype):
        _n[0] += 1
        return nc.alloc_sbuf_tensor_at(f"v{_n[0]}", shape, dtype, offset=base + off).ap()

    def pvw(off, shape, dtype):
        assert off + int(np.prod(shape[1:])) * mybir.dt.size(dtype) <= POOL, (off, shape)
        return _at(pool_addr, off, shape, dtype)

    def xvw(off, shape, dtype):
        assert off + int(np.prod(shape[1:])) * mybir.dt.size(dtype) <= 2 * NHC * T, (off, shape)
        return _at(xi_addr, off, shape, dtype)

    guards = {}

    _pn = [0]

    def palloc(name, shape):
        _pn[0] += 1
        g = nc.psum_tensor(f"ps_{name}_{_pn[0]}", shape, F32)
        t = g.__enter__()
        guards[name] = g
        return t.ap()

    def pfree(*names):
        for n in names:
            guards.pop(n).__exit__(None, None, None)

    ptA = palloc("ptA", [128, 512])
    ptB = palloc("ptB", [128, 512])

    sems = []

    def ctr():
        c = Ctr()
        sems.append(c)
        return c

    CD, CWS, CWL, CWD, CSP, CSR, CG, CC, CV, CS, CT = (ctr() for _ in range(11))
    CSP3 = [ctr() for _ in range(3)]
    COD2 = [ctr() for _ in range(2)]
    CWL2 = [ctr() for _ in range(2)]
    CSR4 = [ctr() for _ in range(4)]
    CWLP = [ctr() for _ in range(4)]
    CWS2 = [ctr() for _ in range(2)]

    def dma(eng, c, out, in_, slow=False):
        c.n += 16
        q[eng].append(lambda e, o=out, i=in_, cc=c, s=slow: e.dma_start(
            out=o, in_=i, allow_slow_non_contiguous=s).then_inc(cc.h, 16))

    def wait(eng, c, val=None):
        v = c.n if val is None else val
        if v > 0:
            q[eng].append(lambda e, cc=c, vv=v: e.wait_ge(cc.h, vv))

    def op(eng, fn, inc=None):
        if inc is None:
            q[eng].append(fn)
        else:
            inc.n += 1
            q[eng].append(lambda e, f=fn, cc=inc: f(e).then_inc(cc.h, 1))

    def drain(eng):
        q[eng].append(lambda e: e.drain())

    def barrier(engines=("sync", "scalar", "vector", "tensor", "gpsimd")):
        snap = [(c, c.n) for c in sems if c.n > 0]
        for eng in engines:
            for c, v in snap:
                wait(eng, c, v)

    def dr3(ap, p=128):
        return ap.rearrange("(c p) h -> c p h", p=p).transpose([1, 0, 2])

    def sb3(ap, h):
        return ap.rearrange("p (c h) -> p c h", h=h)

    dma("sync", CD, sb3(s_x, H), dr3(d_x))
    dma("sync", CD, s_idf, d_idf)
    dma("sync", CD, s_ids, d_ids)
    dma("sync", CD, s_gam, d_gamma.rearrange("(c p) -> c p", p=128).transpose([1, 0]), slow=True)
    dma("sync", CD, s_sig[0:1, 0:1], d_clipx)
    dma("sync", CD, s_sig[0:1, 2:3], d_clipd)
    boot_cd = CD.n

    wait("vector", CD, boot_cd)
    op("vector", lambda e: e.memset(s_ones, 1.0))
    op("vector", lambda e: e.memset(s_swg, 0.0))
    op("vector", lambda e: e.memset(s_swu, 0.0))
    op("vector", lambda e: e.memset(s_sgmax, 0.0))
    op("vector", lambda e: e.memset(s_sc, 0.0))
    op("vector", lambda e: e.memset(s_col, 0.0))
    op("vector", lambda e: e.memset(s_cst[:, 0:1], EPS))
    op("vector", lambda e: e.memset(s_cst[:, 1:2], float(H)))
    op("vector", lambda e: e.memset(s_cst[:, 2:3], 0.25))
    op("vector", lambda e: e.memset(s_cst[:, 3:4], SQH))
    op("vector", lambda e: e.memset(s_sel4, 0.0))
    drain("vector")
    for j in range(4):
        op("vector", lambda e, j=j: e.memset(
            s_sel4[0:1, 128 * j + 32 * j:128 * j + 32 * j + 32], 1.0))
    op("vector", lambda e: e.tensor_copy(out=s_idb, in_=s_idf), inc=CV)
    cv_idb = CV.n

    wait("scalar", CD, boot_cd)
    op("scalar", lambda e: e.activation(out=s_sig[0:1, 1:2], in_=s_sig[0:1, 0:1], func=AF.Sigmoid))
    op("scalar", lambda e: e.activation(out=s_sig[0:1, 3:4], in_=s_sig[0:1, 2:3], func=AF.Sigmoid),
       inc=CS)

    p0_wraw = [pvw(0, [128, 2816], F32), pvw(11264, [128, 2816], F32),
               pvw(22528, [128, 2816], F32), pvw(33792, [128, 2816], F32)]
    p0_w8 = [pvw(45056, [128, 2816], I8), pvw(47872, [128, 2816], I8)]
    p0_wqb = [pvw(50688, [128, 2816], BF16), pvw(56320, [128, 2816], BF16)]
    p0_stg = [pvw(61952, [128, 2816], BF16), pvw(67584, [128, 2816], BF16)]

    qw_state = {}
    ccst = {}

    def quant_all():
        if "row_cv" in ccst:
            wait("tensor", CV, ccst["row_cv"])
        chunks = []
        for d_w, d_wT, s_scales, nrc, ncol, rm in (
                (d_wg, d_wgT, s_swg, NIC, H, True),
                (d_wu, d_wuT, s_swu, NIC, H, True),
                (d_wd, d_wdT, s_swd, NHC, II, False)):
            for cl in range(nrc):
                chunks.append((d_w, d_wT, s_scales, cl, ncol, rm))
        n = len(chunks)
        cs_done, ct_done, sp_done, cv_recip = {}, {}, {}, {}

        def scale_part(ci):
            d_w, d_wT, s_scales, cl, ncol, rm = chunks[ci]
            rb4 = ci % 4
            buf = p0_wraw[rb4]
            sl = ci % 2
            if ci >= 4:
                wait("sync", CS, cs_done[ci - 4])
            dma("sync", CWLP[rb4], buf[:, 0:ncol],
                d_w[128 * cl:128 * (cl + 1), :])
            wait("vector", CWLP[rb4])
            op("vector", lambda e, cl=cl, buf=buf, s_scales=s_scales, ncol=ncol:
               e.tensor_reduce(out=s_scales[:, cl:cl + 1],
                               in_=buf[:, 0:ncol], axis=AX.X,
                               op=ALU.max, apply_absolute_value=True))
            drain("vector")
            if ci >= 2:
                wait("vector", CS, cs_done[ci - 2])
            elif qw_state.get("cs0"):
                wait("vector", CS, qw_state["cs0"])
            op("vector", lambda e, cl=cl, sl=sl, s_scales=s_scales: e.tensor_scalar(
                out=s_t16a[:, sl:sl + 1], in0=s_scales[:, cl:cl + 1],
                scalar1=1.0 / QMAX, scalar2=None, op0=ALU.mult))
            drain("vector")
            op("vector", lambda e, sl=sl: e.reciprocal(
                out=s_t16b[:, sl:sl + 1], in_=s_t16a[:, sl:sl + 1]), inc=CV)
            cv_recip[ci] = CV.n

        def process_part(ci):
            d_w, d_wT, s_scales, cl, ncol, rm = chunks[ci]
            rb4 = ci % 4
            buf = p0_wraw[rb4]
            b = ci % 2
            sl = ci % 2
            w8, qb, stg = p0_w8[b], p0_wqb[b], p0_stg[b]
            ncc = ncol // 128
            wait("scalar", CV, cv_recip[ci])
            if ci >= 2:
                wait("scalar", CT, ct_done[ci - 2])
            op("scalar", lambda e, buf=buf, w8=w8, ncol=ncol, sl=sl: e.activation(
                out=w8[:, 0:ncol], in_=buf[:, 0:ncol],
                func=AF.Identity, scale=s_t16b[:, sl:sl + 1]))
            drain("scalar")
            op("scalar", lambda e, qb=qb, w8=w8, ncol=ncol: e.activation(
                out=qb[:, 0:ncol], in_=w8[:, 0:ncol], func=AF.Identity), inc=CS)
            cs_done[ci] = CS.n
            wait("tensor", CS)
            if ci == 0:
                wait("tensor", CV, cv_idb)
            if ci >= 2:
                wait("vector", CWS2[b], sp_done[ci - 2])
            ngrp = (ncc + 3) // 4
            for g in range(ngrp):
                bank = ptA if g % 2 == 0 else ptB
                key = ("pgcv", g % 2)
                if key in qw_state:
                    wait("tensor", CV, qw_state[key])
                tiles = list(range(4 * g, min(4 * g + 4, ncc)))
                for jj, cc in enumerate(tiles):
                    op("tensor", lambda e, cc=cc, qb=qb, bank=bank, jj=jj:
                       e.matmul(bank[:, jj * 128:jj * 128 + 128],
                                qb[:, cc * 128:cc * 128 + 128],
                                s_idb, start=True, stop=True),
                       inc=(CT if jj == len(tiles) - 1 else None))
                wait("vector", CT)
                w = len(tiles) * 128
                op("vector", lambda e, g=g, stg=stg, bank=bank, w=w: e.tensor_copy(
                    out=stg[:, g * 512:g * 512 + w], in_=bank[:, 0:w]), inc=CV)
                qw_state[key] = CV.n
            ct_done[ci] = CT.n
            wait("gpsimd", CV)
            if rm:
                dma("gpsimd", CWS2[b],
                    d_wT[128 * cl:128 * (cl + 1), 0:ncc * 128],
                    stg[:, 0:ncc * 128])
            else:
                for cc in range(ncc):
                    dma("gpsimd", CWS2[b],
                        d_wT[128 * cc:128 * (cc + 1), 128 * cl:128 * cl + 128],
                        stg[:, cc * 128:cc * 128 + 128])
            sp_done[ci] = CWS2[b].n

        for ci in range(n):
            scale_part(ci)
            if ci >= 1:
                process_part(ci - 1)
        process_part(n - 1)
        gu = (CWS2[0].n, CWS2[1].n)
        allw = gu
        if "ccin" in ccst:
            wait("vector", CG, ccst["ccin"])
        wait("tensor", CV)
        for c in range(NHC):
            op("tensor", lambda e, c=c: e.matmul(ptA[0:1, 0:128], s_swd[:, c:c + 1],
                                                 s_idf, start=True, stop=True), inc=CT)
            wait("vector", CT)
            op("vector", lambda e: e.tensor_copy(out=s_row128, in_=ptA[0:1, 0:128]),
               inc=CV)
            wait("tensor", CV)
            op("tensor", lambda e: e.matmul(ptB[:, 0:128], s_ones[0:1, :], s_row128,
                                            start=True, stop=True), inc=CT)
            wait("vector", CT)
            op("vector", lambda e, c=c: e.tensor_copy(
                out=s_swdb[:, c * 128:(c + 1) * 128], in_=ptB[:, 0:128]), inc=CV)
            wait("tensor", CV)
        return gu, allw

    def ln_stats():
        for tc in range(NTC):
            xc = s_x[:, tc * H:(tc + 1) * H]
            op("vector", lambda e, xc=xc, tc=tc: e.bn_stats(
                out=s_stats[:, tc * 12:tc * 12 + 6], in_=xc[:, 0:512]))
            op("vector", lambda e, xc=xc, tc=tc: e.bn_stats(
                out=s_stats[:, tc * 12 + 6:tc * 12 + 12], in_=xc[:, 512:1024]))
        drain("vector")
        v_m = s_stats.rearrange("p (t g c) -> p t g c", g=4, c=3)[:, :, :, 1:2] \
                     .rearrange("p t g o -> p t (g o)")
        v_M2 = s_stats.rearrange("p (t g c) -> p t g c", g=4, c=3)[:, :, :, 2:3] \
                      .rearrange("p t g o -> p t (g o)")
        v_mflat = s_stats.rearrange("p (a c) -> p a c", c=3)[:, :, 1:2] \
                         .rearrange("p a o -> p (a o)")
        op("vector", lambda e: e.tensor_reduce(out=s_m16, in_=v_m, axis=AX.X, op=ALU.add))
        op("vector", lambda e: e.tensor_reduce(out=s_v16, in_=v_M2, axis=AX.X, op=ALU.add))
        op("vector", lambda e: e.tensor_tensor(out=s_msq, in0=v_mflat, in1=v_mflat, op=ALU.mult))
        drain("vector")
        v_msq = s_msq.rearrange("p (t g) -> p t g", g=4)
        op("vector", lambda e: e.tensor_reduce(out=s_t16a, in_=v_msq, axis=AX.X, op=ALU.add))
        op("vector", lambda e: e.tensor_scalar(out=s_m16, in0=s_m16, scalar1=0.25,
                                               scalar2=None, op0=ALU.mult))
        op("vector", lambda e: e.tensor_scalar(out=s_v16, in0=s_v16, scalar1=1.0 / H,
                                               scalar2=None, op0=ALU.mult))
        drain("vector")
        op("vector", lambda e: e.tensor_scalar(out=s_t16b, in0=s_t16a, scalar1=0.25,
                                               scalar2=None, op0=ALU.mult))
        op("vector", lambda e: e.tensor_tensor(out=s_t16c, in0=s_m16, in1=s_m16, op=ALU.mult))
        drain("vector")
        op("vector", lambda e: e.tensor_tensor(out=s_v16, in0=s_v16, in1=s_t16b, op=ALU.add))
        drain("vector")
        op("vector", lambda e: e.tensor_tensor(out=s_v16, in0=s_v16, in1=s_t16c, op=ALU.subtract),
           inc=CV)
        drain("vector")

    def cross_core_max_pre(src, ncols, ag_idx):
        drain("vector")
        op("vector", lambda e, src=src, ncols=ncols: e.tensor_reduce(
            out=s_red1[:, 0:1], in_=src[:, 0:ncols], axis=AX.X, op=ALU.max), inc=CV)
        wait("tensor", CV)
        op("tensor", lambda e: e.matmul(ptA[0:1, 0:128], s_red1[:, 0:1], s_idf,
                                        start=True, stop=True), inc=CT)
        wait("vector", CT)
        op("vector", lambda e: e.tensor_copy(out=s_row128, in_=ptA[0:1, 0:128]))
        drain("vector")
        op("vector", lambda e: e.tensor_reduce(out=s_sc[0:1, 14:15], in_=s_row128,
                                               axis=AX.X, op=ALU.max))
        drain("vector")
        op("vector", lambda e: e.tensor_copy(
            out=s_row128, in_=s_sc[0:1, 14:15].broadcast_to([1, 128])), inc=CV)
        ccst["row_cv"] = CV.n
        wait("gpsimd", CV)
        dma("gpsimd", CG, d_ccin[ag_idx], s_row128)
        ccst["ccin"] = CG.n
        wait("gpsimd", CG)
        op("gpsimd", lambda e, i=ag_idx: e.collective_compute(
            "AllGather", ALU.bypass, replica_groups=[list(range(8))],
            ins=[d_ccin[i].opt()], outs=[d_ccout[i].opt()]), inc=CC)
        wait("gpsimd", CC)
        dma("gpsimd", CG, s_g8, d_ccout[ag_idx])

    def cross_core_max_post(out_slot):
        wait("vector", CG)
        op("vector", lambda e: e.tensor_reduce(out=s_red8[:, 0:1], in_=s_g8,
                                               axis=AX.X, op=ALU.max), inc=CV)
        wait("tensor", CV)
        op("tensor", lambda e: e.matmul(ptA[0:1, 0:8], s_red8[0:8, 0:1], s_idf[0:8, 0:8],
                                        start=True, stop=True), inc=CT)
        wait("vector", CT)
        op("vector", lambda e: e.tensor_copy(out=s_row128[0:1, 0:8], in_=ptA[0:1, 0:8]))
        drain("vector")
        op("vector", lambda e, o=out_slot: e.tensor_reduce(
            out=s_sc[0:1, o:o + 1], in_=s_row128[0:1, 0:8], axis=AX.X, op=ALU.max), inc=CV)
        drain("vector")

    def cross_core_max(src, ncols, ag_idx, out_slot):
        cross_core_max_pre(src, ncols, ag_idx)
        cross_core_max_post(out_slot)

    def bcast_scalar(src_slot, dst_col):
        wait("tensor", CV)
        op("tensor", lambda e, s=src_slot: e.matmul(
            ptB[:, 0:1], s_ones[0:1, :], s_sc[0:1, s:s + 1], start=True, stop=True), inc=CT)
        wait("vector", CT)
        op("vector", lambda e, d=dst_col: e.tensor_copy(
            out=s_col[:, d:d + 1], in_=ptB[:, 0:1]), inc=CV)
        drain("vector")
        wait("tensor", CV)

    def ffn_pre(call):
        ag0 = 0 if call == 0 else 2
        ln_stats()
        wait("scalar", CV)
        op("scalar", lambda e: e.activation(out=s_t16b, in_=s_v16, func=AF.Sqrt,
                                            bias=s_cst[:, 0:1]), inc=CS)
        wait("vector", CS)
        op("vector", lambda e: e.reciprocal(out=s_r16, in_=s_t16b))
        drain("vector")
        op("vector", lambda e: e.tensor_tensor(out=s_t16a, in0=s_m16, in1=s_r16,
                                               op=ALU.mult))
        drain("vector")
        op("vector", lambda e: e.tensor_scalar(out=s_b16, in0=s_t16a, scalar1=-1.0,
                                               scalar2=None, op0=ALU.mult), inc=CV)
        wait("scalar", CV)
        yred = []
        for tc in range(NTC):
            pp = tc % 2
            if tc >= 2:
                wait("scalar", CV, yred[tc - 2])
            op("scalar", lambda e, tc=tc, pp=pp: e.activation(
                out=s_y[:, pp * H:(pp + 1) * H], in_=s_x[:, tc * H:(tc + 1) * H],
                func=AF.Identity, scale=s_r16[:, tc:tc + 1],
                bias=s_b16[:, tc:tc + 1]), inc=CS)
            wait("vector", CS)
            op("vector", lambda e, tc=tc, pp=pp: e.tensor_reduce(
                out=s_ymax[:, tc:tc + 1], in_=s_y[:, pp * H:(pp + 1) * H],
                axis=AX.X, op=ALU.max, apply_absolute_value=True), inc=CV)
            yred.append(CV.n)
        drain("vector")
        cross_core_max_pre(s_ymax, 16, ag0)

    def ffn(call, pre_done=False):
        ag0 = 0 if call == 0 else 2
        wgbuf = [pvw(0, [128, 2048], BF16), pvw(4096, [128, 2048], BF16)]
        wubuf = [pvw(8192, [128, 2048], BF16), pvw(12288, [128, 2048], BF16)]
        sgt = [pvw(16384 + 8192 * k, [128, 2048], F32) for k in range(3)]
        slb = [pvw(40960 + 2048 * k, [128, 512], F32) for k in range(2)]
        wdTb = pvw(0, [128, NIC * 1024], BF16)
        hintb = [pvw(45056, [128, NIC * 512], BF16), xvw(0, [128, NIC * 512], BF16)]
        sgrl = [pvw(67584 + 2048 * k, [128, 512], F32) for k in range(4)]
        tdq = [pvw(75776 + 4096 * k, [128, 1024], F32) for k in range(2)]

        if not pre_done:
            ffn_pre(call)
        cross_core_max_post(0)
        op("vector", lambda e: e.tensor_tensor(out=s_sc[0:1, 1:2], in0=s_sc[0:1, 0:1],
                                               in1=s_sig[0:1, 1:2], op=ALU.mult))
        drain("vector")
        op("vector", lambda e: e.tensor_scalar(out=s_sc[0:1, 2:3], in0=s_sc[0:1, 1:2],
                                               scalar1=1.0 / QMAX, scalar2=None, op0=ALU.mult))
        drain("vector")
        op("vector", lambda e: e.reciprocal(out=s_sc[0:1, 3:4], in_=s_sc[0:1, 2:3]), inc=CV)
        drain("vector")
        bcast_scalar(3, 0)
        bcast_scalar(2, 1)
        op("vector", lambda e: e.tensor_scalar(out=s_msq[:, 0:24], in0=s_swg,
                                               scalar1=s_col[:, 1:2], scalar2=None, op0=ALU.mult))
        op("vector", lambda e: e.tensor_scalar(out=s_msq[:, 24:48], in0=s_swu,
                                               scalar1=s_col[:, 1:2], scalar2=None, op0=ALU.mult))
        drain("vector")
        op("vector", lambda e: e.tensor_scalar(out=s_deqg, in0=s_msq[:, 0:24],
                                               scalar1=1.0 / QMAX, scalar2=None, op0=ALU.mult))
        op("vector", lambda e: e.tensor_scalar(out=s_dequ, in0=s_msq[:, 24:48],
                                               scalar1=1.0 / QMAX, scalar2=None, op0=ALU.mult))
        op("vector", lambda e: e.tensor_scalar(out=s_a16, in0=s_r16, scalar1=s_col[:, 0:1],
                                               scalar2=None, op0=ALU.mult))
        drain("vector")
        op("vector", lambda e: e.tensor_tensor(out=s_t16a, in0=s_m16, in1=s_a16, op=ALU.mult))
        drain("vector")
        op("vector", lambda e: e.tensor_scalar(out=s_b16, in0=s_t16a, scalar1=-1.0,
                                               scalar2=None, op0=ALU.mult), inc=CV)
        wait("scalar", CV)
        ct_tp = []
        st4 = {0: CV.n, 1: CV.n}
        for tc in range(NTC):
            pp = tc % 2
            op("scalar", lambda e, tc=tc, pp=pp: e.activation(
                out=s_q8[:, pp * H:(pp + 1) * H], in_=s_x[:, tc * H:(tc + 1) * H],
                func=AF.Identity, scale=s_a16[:, tc:tc + 1], bias=s_b16[:, tc:tc + 1]))
            drain("scalar")
            if tc >= 2:
                wait("scalar", CT, ct_tp[tc - 2])
            op("scalar", lambda e, pp=pp: e.activation(
                out=s_qb[:, pp * H:(pp + 1) * H], in_=s_q8[:, pp * H:(pp + 1) * H],
                func=AF.Identity), inc=CS)
            wait("tensor", CS)
            for g in range(2):
                bank = ptA if g == 0 else ptB
                wait("tensor", CV, st4[g])
                for j in range(4):
                    k = g * 4 + j
                    op("tensor", lambda e, k=k, pp=pp, bank=bank, j=j: e.matmul(
                        bank[:, j * 128:j * 128 + 128],
                        s_qb[:, pp * H + k * 128:pp * H + k * 128 + 128],
                        s_idb, start=True, stop=True), inc=(CT if j == 3 else None))
                wait("vector", CT)
                for j in range(4):
                    k = g * 4 + j
                    op("vector", lambda e, k=k, tc=tc, bank=bank, j=j: e.tensor_copy(
                        out=s_xiT[:, k * T + tc * 128:k * T + tc * 128 + 128],
                        in_=bank[:, j * 128:j * 128 + 128]), inc=(CV if j == 3 else None))
                st4[g] = CV.n
            ct_tp.append(CT.n)
        cv_xiT = CV.n
        if call == 0:
            wait("sync", CWS2[0], cws_all[0])
            wait("sync", CWS2[1], cws_all[1])
        pg = [palloc("pg0", [128, 512]), palloc("pg1", [128, 512])]
        pu = [palloc("pu0", [128, 512]), palloc("pu1", [128, 512])]
        cwl_piece, ct_piece = [], []

        def piece_dma(pc):
            b = pc % 2
            dma("sync", CWL2[b], sb3(wgbuf[b], 1024), dr3(d_wgT[256 * pc:256 * (pc + 1), :]))
            dma("sync", CWL2[b], sb3(wubuf[b], 1024), dr3(d_wuT[256 * pc:256 * (pc + 1), :]))
            cwl_piece.append(CWL2[b].n)

        piece_dma(0)
        wait("tensor", CV, cv_xiT)
        ge, spl, sgred = [], [], []
        for i in range(NIC):
            pc, sub = i // 2, i % 2
            if sub == 0:
                if pc + 1 < 11:
                    if pc >= 1:
                        wait("sync", CT, ct_piece[pc - 1])
                    piece_dma(pc + 1)
                wait("tensor", CWL2[pc % 2], cwl_piece[pc])
            if i >= 3:
                wait("vector", CSP3[i % 3], spl[i - 3])
            for t4 in range(4):
                n = i * 4 + t4
                bg, bu = pg[t4 % 2], pu[t4 % 2]
                if n >= 2:
                    wait("tensor", CV, ge[n - 2])
                for k in range(NHC):
                    op("tensor", lambda e, k=k, sub=sub, b=pc % 2, t4=t4, bg=bg: e.matmul(
                        bg, wgbuf[b][:, sub * 1024 + k * 128:sub * 1024 + k * 128 + 128],
                        s_xiT[:, k * T + t4 * 512:k * T + t4 * 512 + 512],
                        start=(k == 0), stop=(k == NHC - 1)),
                       inc=(CT if k == NHC - 1 else None))
                wait("scalar", CT)
                if n >= 2:
                    wait("scalar", CV, ge[n - 2])
                op("scalar", lambda e, i=i, t4=t4, bg=bg: e.activation(
                    out=slb[t4 % 2], in_=bg, func=AF.Silu, scale=s_deqg[:, i:i + 1]), inc=CS)
                for k in range(NHC):
                    op("tensor", lambda e, k=k, sub=sub, b=pc % 2, t4=t4, bu=bu: e.matmul(
                        bu, wubuf[b][:, sub * 1024 + k * 128:sub * 1024 + k * 128 + 128],
                        s_xiT[:, k * T + t4 * 512:k * T + t4 * 512 + 512],
                        start=(k == 0), stop=(k == NHC - 1)),
                       inc=(CT if k == NHC - 1 else None))
                wait("vector", CT)
                wait("vector", CS)
                op("vector", lambda e, i=i, t4=t4, bu=bu: e.scalar_tensor_tensor(
                    out=sgt[i % 3][:, t4 * 512:t4 * 512 + 512], in0=bu,
                    scalar=s_dequ[:, i:i + 1], in1=slb[t4 % 2],
                    op0=ALU.mult, op1=ALU.mult), inc=CV)
                ge.append(CV.n)
            ct_piece_val = CT.n
            if sub == 1 or i == NIC - 1:
                ct_piece.append(ct_piece_val)
            drain("vector")
            op("vector", lambda e, i=i: e.tensor_reduce(
                out=s_sgmax[:, i:i + 1], in_=sgt[i % 3], axis=AX.X,
                op=ALU.max, apply_absolute_value=True), inc=CV)
            sgred.append(CV.n)
            wait("sync", CV)
            dma("sync", CSP3[i % 3], d_sg[128 * i:128 * (i + 1), :], sgt[i % 3])
            spl.append(CSP3[i % 3].n)
        ct_gate = CT.n
        pfree("pu1", "pu0", "pg1", "pg0")
        wait("sync", CT, ct_gate)
        for c3 in CSP3:
            wait("sync", c3)
        wait("sync", CV)
        if call == 0:
            wait("sync", CWS2[0], cws_all[0])
            wait("sync", CWS2[1], cws_all[1])
        wd_batch = []
        for k4 in range(4):
            lo, hi = 6 * k4, min(6 * k4 + 6, NIC)
            for i in range(lo, hi):
                dma("sync", CWLP[k4], wdTb[:, 1024 * i:1024 * (i + 1)],
                    d_wdT[128 * i:128 * (i + 1), :])
            wd_batch.append(CWLP[k4].n)
        cross_core_max(s_sgmax, 22, ag0 + 1, 4)
        op("vector", lambda e: e.tensor_tensor(out=s_sc[0:1, 5:6], in0=s_sc[0:1, 4:5],
                                               in1=s_sig[0:1, 3:4], op=ALU.mult))
        drain("vector")
        op("vector", lambda e: e.tensor_scalar(out=s_sc[0:1, 6:7], in0=s_sc[0:1, 5:6],
                                               scalar1=1.0 / QMAX, scalar2=None, op0=ALU.mult))
        drain("vector")
        op("vector", lambda e: e.reciprocal(out=s_sc[0:1, 7:8], in_=s_sc[0:1, 6:7]))
        drain("vector")
        op("vector", lambda e: e.tensor_scalar(out=s_sc[0:1, 8:9], in0=s_sc[0:1, 6:7],
                                               scalar1=0.5 / QMAX, scalar2=None, op0=ALU.mult), inc=CV)
        drain("vector")
        bcast_scalar(7, 2)
        bcast_scalar(8, 3)
        op("vector", lambda e: e.tensor_scalar(out=s_swdb2, in0=s_swdb, scalar1=s_col[:, 3:4],
                                               scalar2=None, op0=ALU.mult), inc=CV)
        cv_swdb2 = CV.n
        pd = [palloc("pd0", [128, 1024]), palloc("pd1", [128, 1024])]
        de, outd = [], []
        ct_down = []
        hq_cs, hq_cv = [], []
        wait("scalar", CV, cv_swdb2)
        for t4 in range(4):
            hb = hintb[t4 % 2]
            if t4 >= 2:
                wait("vector", CT, ct_down[t4 - 2])
            for i in range(NIC):
                gi = t4 * NIC + i
                rb = sgrl[gi % 4]
                if gi >= 4:
                    wait("sync", CS, hq_cs[gi - 4])
                dma("sync", CSR4[gi % 4], rb,
                    d_sg[128 * i:128 * (i + 1), t4 * 512:t4 * 512 + 512])
                wait("scalar", CSR4[gi % 4])
                if gi >= 2:
                    wait("scalar", CV, hq_cv[gi - 2])
                op("scalar", lambda e, rb=rb, pp=i % 2: e.activation(
                    out=s_q8[:, pp * 512:pp * 512 + 512], in_=rb, func=AF.Identity,
                    scale=s_col[:, 2:3]), inc=CS)
                hq_cs.append(CS.n)
                wait("vector", CS)
                op("vector", lambda e, i=i, pp=i % 2, hb=hb: e.tensor_copy(
                    out=hb[:, i * 512:(i + 1) * 512],
                    in_=s_q8[:, pp * 512:pp * 512 + 512]), inc=CV)
                hq_cv.append(CV.n)
            for m in range(4):
                tc = t4 * 4 + m
                n = t4 * 4 + m
                bank = pd[m % 2]
                if n >= 2:
                    wait("tensor", CV, de[n - 2])
                for i in range(NIC):
                    if m == 0:
                        wait("tensor", CV, hq_cv[t4 * NIC + i])
                    if t4 == 0 and m == 0:
                        wait("tensor", CWLP[i // 6], wd_batch[i // 6])
                    for half in range(2):
                        op("tensor", lambda e, i=i, m=m, half=half, bank=bank, hb=hb: e.matmul(
                            bank[:, half * 512:half * 512 + 512],
                            hb[:, i * 512 + m * 128:i * 512 + m * 128 + 128],
                            wdTb[:, i * 1024 + half * 512:i * 1024 + half * 512 + 512],
                            start=(i == 0), stop=(i == NIC - 1)),
                           inc=(CT if (i == NIC - 1 and half == 1) else None))
                wait("vector", CT)
                wait("vector", CV, cv_swdb2)
                op("vector", lambda e, m=m, bank=bank: e.tensor_tensor(
                    out=tdq[m % 2], in0=bank, in1=s_swdb2, op=ALU.mult))
                drain("vector")
                if call == 0:
                    op("vector", lambda e, tc=tc, m=m: e.tensor_tensor(
                        out=s_x[:, tc * H:(tc + 1) * H], in0=s_x[:, tc * H:(tc + 1) * H],
                        in1=tdq[m % 2], op=ALU.add), inc=CV)
                else:
                    if n >= 2:
                        wait("vector", COD2[m % 2], outd[n - 2])
                    op("vector", lambda e, tc=tc, m=m: e.tensor_tensor(
                        out=s_y[:, (m % 2) * H:(m % 2 + 1) * H],
                        in0=s_x[:, tc * H:(tc + 1) * H],
                        in1=tdq[m % 2], op=ALU.add), inc=CV)
                    wait("sync", CV)
                    dma("sync", COD2[m % 2], d_out[128 * tc:128 * (tc + 1), :],
                        s_y[:, (m % 2) * H:(m % 2 + 1) * H])
                    outd.append(COD2[m % 2].n)
                de.append(CV.n)
                drain("vector")
            ct_down.append(CT.n)
        pfree("pd1", "pd0")

    def attention():
        wqpb = pvw(0, [128, NHC * 256], F32)
        wkvb = pvw(8192, [128, NHC * 256], F32)
        woutb = pvw(16384, [128, 2 * H], F32)
        normT = pvw(24576, [128, NHC * 512], F32)
        qTp = [pvw(40960, [128, 2048], F32), pvw(49152, [128, 2048], F32)]
        outp = [pvw(57344, [128, 2048], F32), pvw(65536, [128, 2048], F32)]
        qtq = pvw(73728, [128, 1024], F32)
        fq2 = [pvw(77824, [128, 256], F32), pvw(78848, [128, 256], F32)]
        fq2Tb = [pvw(79872, [128, 1024], F32), xvw(8192, [128, 1024], F32)]
        qhb = [xvw(20928, [32, 512], F32), xvw(12288, [32, 512], F32)]
        sdenAB = [pvw(24576, [128, 2048], F32), s_y[:, 0:2048]]
        sdenC = xvw(0, [128, 2048], F32)

        def den_slot(h):
            if h % 4 < 3:
                return sdenAB[h // 4], 32 * (h % 4)
            return sdenC, 32 * (h // 4)
        outs = [pvw(32768, [128, 2048], F32), pvw(75776, [128, 2048], F32)]
        k_tok = xvw(0, [128, 2048], F32)
        v_tok = xvw(8192, [128, 2048], F32)
        kvstg = xvw(16384, [128, 512], F32)
        fkb = [xvw(18432, [128, 274], F32), xvw(19584, [128, 274], F32),
               xvw(27072, [128, 274], F32), xvw(28224, [128, 274], F32)]
        vpb = [xvw(20736, [128, 18], F32), xvw(20832, [128, 18], F32),
               xvw(29376, [128, 18], F32), xvw(29472, [128, 18], F32)]
        kvk = [xvw(22976 + 160 * h, [32, 34], F32) for h in range(8)]
        kvx2a = [xvw(24256 + 160 * h, [128, 34], F32) for h in range(8)]
        kvx2b = [xvw(25536 + 160 * h, [128, 34], F32) for h in range(8)]
        ccol = [xvw(26816 + 32 * h, [32, 4], F32) for h in range(8)]

        ln_stats()
        wait("scalar", CV)
        op("scalar", lambda e: e.activation(out=s_t16b, in_=s_v16, func=AF.Sqrt, bias=s_cst[:, 0:1]))
        drain("scalar")
        op("scalar", lambda e: e.activation(out=s_t16c, in_=s_v16, func=AF.Sqrt,
                                            scale=s_cst[:, 1:2]), inc=CS)
        wait("vector", CS)
        op("vector", lambda e: e.reciprocal(out=s_r16, in_=s_t16b))
        drain("vector")
        op("vector", lambda e: e.tensor_tensor(out=s_t16a, in0=s_t16c, in1=s_r16, op=ALU.mult))
        drain("vector")
        op("vector", lambda e: e.tensor_scalar(out=s_t16a, in0=s_t16a, scalar1=1e-12,
                                               scalar2=None, op0=ALU.max))
        drain("vector")
        op("vector", lambda e: e.reciprocal(out=s_t16b, in_=s_t16a))
        drain("vector")
        op("vector", lambda e: e.tensor_scalar(out=s_t16b, in0=s_t16b, scalar1=32.0,
                                               scalar2=None, op0=ALU.mult))
        drain("vector")
        op("vector", lambda e: e.tensor_tensor(out=s_a16, in0=s_t16b, in1=s_r16, op=ALU.mult))
        drain("vector")
        op("vector", lambda e: e.tensor_tensor(out=s_t16a, in0=s_m16, in1=s_a16, op=ALU.mult))
        drain("vector")
        op("vector", lambda e: e.tensor_scalar(out=s_b16, in0=s_t16a, scalar1=-1.0,
                                               scalar2=None, op0=ALU.mult), inc=CV)
        cv_nrm = CV.n
        dma("sync", CD, sb3(wqpb, 256), dr3(d_wqp))
        dma("sync", CD, sb3(wkvb, 256), dr3(d_wkv))
        dma("sync", CD, sb3(woutb, H), dr3(d_woutp))
        cd_aw = CD.n
        wait("vector", CD, cd_aw)
        for c in range(NHC):
            op("vector", lambda e, c=c: e.tensor_scalar(
                out=wqpb[:, c * 256:(c + 1) * 256], in0=wqpb[:, c * 256:(c + 1) * 256],
                scalar1=s_gam[:, c:c + 1], scalar2=None, op0=ALU.mult))
        for c in range(NHC):
            op("vector", lambda e, c=c: e.tensor_scalar(
                out=wkvb[:, c * 256:(c + 1) * 256], in0=wkvb[:, c * 256:(c + 1) * 256],
                scalar1=s_gam[:, c:c + 1], scalar2=None, op0=ALU.mult))
        op("vector", lambda e: e.memset(outp[0], 0.0))
        op("vector", lambda e: e.memset(outp[1], 0.0), inc=CV)
        cv_gw = CV.n
        pq = [palloc("pq0", [128, 512]), palloc("pq1", [128, 512])]
        pkv = [palloc("pk", [128, 512]), palloc("pv_", [128, 512])]
        wait("scalar", CV, cv_nrm)
        ev_prev = None
        s2cv = {}
        for tq in range(4):
            if ev_prev is not None:
                wait("tensor", CS, ev_prev[0])
                wait("tensor", CV, ev_prev[1])
            ct_tpj = []
            for j in range(4):
                tc = tq * 4 + j
                pp = j % 2
                if j >= 2:
                    wait("scalar", CT, ct_tpj[j - 2])
                op("scalar", lambda e, tc=tc, pp=pp: e.activation(
                    out=s_y[:, pp * H:(pp + 1) * H], in_=s_x[:, tc * H:(tc + 1) * H],
                    func=AF.Identity, scale=s_a16[:, tc:tc + 1], bias=s_b16[:, tc:tc + 1]), inc=CS)
                wait("tensor", CS)
                for g in range(2):
                    bank = ptA if g == 0 else ptB
                    if g in s2cv:
                        wait("tensor", CV, s2cv[g])
                    for jj in range(4):
                        k = g * 4 + jj
                        op("tensor", lambda e, k=k, pp=pp, bank=bank, jj=jj: e.matmul(
                            bank[:, jj * 128:jj * 128 + 128],
                            s_y[:, pp * H + k * 128:pp * H + k * 128 + 128],
                            s_idf, start=True, stop=True), inc=(CT if jj == 3 else None))
                    wait("vector", CT)
                    for jj in range(4):
                        k = g * 4 + jj
                        op("vector", lambda e, k=k, j=j, bank=bank, jj=jj: e.tensor_copy(
                            out=normT[:, k * 512 + j * 128:k * 512 + j * 128 + 128],
                            in_=bank[:, jj * 128:jj * 128 + 128]), inc=(CV if jj == 3 else None))
                    s2cv[g] = CV.n
                ct_tpj.append(CT.n)
            cv_nq = CV.n
            wait("tensor", CV, cv_nq)
            if tq == 0:
                wait("tensor", CV, cv_gw)
            for k in range(NHC):
                for w in range(2):
                    op("tensor", lambda e, k=k, w=w: e.matmul(
                        pq[w], wqpb[:, k * 256 + w * 128:k * 256 + w * 128 + 128],
                        normT[:, k * 512:k * 512 + 512],
                        start=(k == 0), stop=(k == NHC - 1)),
                       inc=(CT if (k == NHC - 1 and w == 1) else None))
                for w in range(2):
                    op("tensor", lambda e, k=k, w=w: e.matmul(
                        pkv[w], wkvb[:, k * 256 + w * 128:k * 256 + w * 128 + 128],
                        normT[:, k * 512:k * 512 + 512],
                        start=(k == 0), stop=(k == NHC - 1)),
                       inc=(CT if (k == NHC - 1 and w == 1) else None))
            wait("scalar", CT)
            op("scalar", lambda e, tq=tq: e.activation(
                out=qTp[0][:, tq * 512:tq * 512 + 512], in_=pq[0], func=AF.Identity,
                scale=s_cst[:, 2:3]))
            drain("scalar")
            op("scalar", lambda e, tq=tq: e.activation(
                out=qTp[1][:, tq * 512:tq * 512 + 512], in_=pq[1], func=AF.Identity,
                scale=s_cst[:, 2:3]), inc=CS)
            cs_qe = CS.n
            wait("vector", CT)
            for kv_i, dst in ((0, k_tok), (1, v_tok)):
                op("vector", lambda e, kv_i=kv_i: e.tensor_copy(out=kvstg, in_=pkv[kv_i]),
                   inc=CV)
                wait("tensor", CV)
                bank = ptA if kv_i == 0 else ptB
                for j in range(4):
                    op("tensor", lambda e, j=j, bank=bank: e.matmul(
                        bank[:, j * 128:j * 128 + 128], kvstg[:, j * 128:j * 128 + 128],
                        s_idf, start=True, stop=True), inc=(CT if j == 3 else None))
                wait("vector", CT)
                for j in range(4):
                    tc = tq * 4 + j
                    op("vector", lambda e, tc=tc, j=j, bank=bank, dst=dst: e.tensor_copy(
                        out=dst[:, tc * 128:tc * 128 + 128],
                        in_=bank[:, j * 128:j * 128 + 128]), inc=(CV if j == 3 else None))
                wait("tensor", CV)
            ev_prev = (cs_qe, CV.n)
        pfree("pv_", "pk", "pq1", "pq0")
        pkvp = palloc("pkvp", [32, 512])
        wait("vector", CS)
        for bz in range(4):
            op("vector", lambda e, bz=bz: e.memset(fkb[bz][:, 0:1], 1.0))
            op("vector", lambda e, bz=bz: e.memset(vpb[bz][:, 16:17], 1.0),
               inc=(CV if bz == 3 else None))
        wait("scalar", CV)
        ct_kvp = []
        for h in range(8):
            for tc in range(NTC):
                gtc = h * NTC + tc
                b = gtc % 4
                if gtc >= 4:
                    wait("scalar", CT, ct_kvp[gtc - 4])
                    wait("vector", CT, ct_kvp[gtc - 4])
                op("scalar", lambda e, h=h, tc=tc, b=b: e.activation(
                    out=fkb[b][:, 1:17],
                    in_=k_tok[:, tc * 128 + 16 * h:tc * 128 + 16 * h + 16],
                    func=AF.Identity))
                op("scalar", lambda e, h=h, tc=tc, b=b: e.activation(
                    out=vpb[b][:, 0:16],
                    in_=v_tok[:, tc * 128 + 16 * h:tc * 128 + 16 * h + 16],
                    func=AF.Identity), inc=CS)
                ktile = k_tok[:, tc * 128 + 16 * h:tc * 128 + 16 * h + 16]
                in0 = ktile.unsqueeze(2).broadcast_to([128, 16, 16])
                in1 = ktile.unsqueeze(1).broadcast_to([128, 16, 16])
                o3 = fkb[b][:, 17:273].rearrange("p (a bb) -> p a bb", a=16)
                op("vector", lambda e, in0=in0, in1=in1, o3=o3: e.tensor_tensor(
                    out=o3, in0=in0, in1=in1, op=ALU.mult), inc=CV)
                wait("tensor", CV)
                wait("tensor", CS)
                op("tensor", lambda e, tc=tc, b=b: e.matmul(
                    pkvp[0:17, 0:273], vpb[b][:, 0:17], fkb[b][:, 0:273],
                    start=(tc == 0), stop=(tc == NTC - 1)), inc=CT)
                ct_kvp.append(CT.n)
            wait("scalar", CT)
            op("scalar", lambda e: e.activation(out=kvstg[0:17, 0:17],
                                                in_=pkvp[0:17, 0:17], func=AF.Identity))
            op("scalar", lambda e: e.activation(out=kvstg[0:17, 17:273],
                                                in_=pkvp[0:17, 17:273], func=AF.Identity,
                                                scale=s_cst[0:17, 3:4]), inc=CS)
            wait("tensor", CS)
            op("tensor", lambda e: e.matmul(ptA[0:16, 0:17], kvstg[0:17, 1:17],
                                            s_idf[0:17, 0:17], start=True, stop=True))
            op("tensor", lambda e: e.matmul(ptA[0:128, 32:49], kvstg[0:17, 17:145],
                                            s_idf[0:17, 0:17], start=True, stop=True))
            op("tensor", lambda e: e.matmul(ptA[0:128, 64:81], kvstg[0:17, 145:273],
                                            s_idf[0:17, 0:17], start=True, stop=True), inc=CT)
            if h == 0:
                for hz in range(8):
                    op("vector", lambda e, hz=hz: e.memset(kvk[hz], 0.0))
                    op("vector", lambda e, hz=hz: e.memset(kvx2a[hz], 0.0))
                    op("vector", lambda e, hz=hz: e.memset(kvx2b[hz], 0.0))
                    op("vector", lambda e, hz=hz: e.memset(ccol[hz], 0.0),
                       inc=(CV if hz == 7 else None))
                wait("scalar", CV)
            wait("scalar", CT)
            op("scalar", lambda e, h=h: e.activation(out=kvk[h][0:16, 0:16],
                                                     in_=ptA[0:16, 0:16], func=AF.Identity))
            op("scalar", lambda e, h=h: e.activation(out=kvk[h][0:16, 32:33],
                                                     in_=ptA[0:16, 16:17], func=AF.Identity))
            op("scalar", lambda e, h=h: e.activation(out=kvx2a[h][:, 0:16],
                                                     in_=ptA[0:128, 32:48], func=AF.Identity))
            op("scalar", lambda e, h=h: e.activation(out=kvx2a[h][:, 32:33],
                                                     in_=ptA[0:128, 48:49], func=AF.Identity))
            op("scalar", lambda e, h=h: e.activation(out=kvx2b[h][:, 0:16],
                                                     in_=ptA[0:128, 64:80], func=AF.Identity))
            op("scalar", lambda e, h=h: e.activation(out=kvx2b[h][:, 32:33],
                                                     in_=ptA[0:128, 80:81], func=AF.Identity))
            op("scalar", lambda e, h=h: e.activation(out=ccol[h][0:17, 0:1],
                                                     in_=kvstg[0:17, 0:1], func=AF.Identity),
               inc=CS)
            wait("tensor", CS)
        pfree("pkvp")
        ptC = palloc("ptC", [128, 512])
        ptD = palloc("ptD", [128, 512])
        po16 = [palloc("po0", [33, 512]), palloc("po1", [33, 512])]
        ob = []
        ct_den = []
        nb = 0
        ob_v = []
        ctfq_last = []
        cs_evac = []
        for tq in range(4):
            wait("tensor", CS)
            qcv = {}
            for j in range(4):
                tc = tq * 4 + j
                bpq = [(ptA, ptB), (ptC, ptD)][j % 2]
                if j % 2 in qcv:
                    wait("tensor", CS, qcv[j % 2])
                for w in range(2):
                    bank = bpq[w]
                    op("tensor", lambda e, tc=tc, w=w, bank=bank: e.matmul(
                        bank[:, 0:128],
                        qTp[w][:, tc * 128:tc * 128 + 128],
                        s_idf, start=True, stop=True), inc=(CT if w == 1 else None))
                wait("scalar", CT)
                for w in range(2):
                    bank = bpq[w]
                    op("scalar", lambda e, j=j, w=w, bank=bank: e.activation(
                        out=qtq[:, j * 256 + w * 128:j * 256 + w * 128 + 128],
                        in_=bank[:, 0:128],
                        func=AF.Identity), inc=(CS if w == 1 else None))
                qcv[j % 2] = CS.n
            cs_qtq = CS.n
            wait("vector", CS, cs_qtq)
            for h in range(8):
                ff = nb % 2
                fq2T = fq2Tb[ff]
                qh = qhb[ff]
                if nb >= 1:
                    wait("vector", CT, ctfq_last[nb - 1])
                if nb >= 2:
                    wait("scalar", CT, ct_den[nb - 2])
                ct_fq = []
                for j in range(4):
                    b = j % 2
                    qsl = qtq[:, j * 256 + 32 * h:j * 256 + 32 * h + 16]
                    in0 = qsl.unsqueeze(2).broadcast_to([128, 16, 16])
                    in1 = qsl.unsqueeze(1).broadcast_to([128, 16, 16])
                    if j >= 2:
                        wait("vector", CT, ct_fq[j - 2])
                    o3 = fq2[b].rearrange("p (a bb) -> p a bb", a=16)
                    op("vector", lambda e, in0=in0, in1=in1, o3=o3: e.tensor_tensor(
                        out=o3, in0=in0, in1=in1, op=ALU.mult), inc=CV)
                    wait("tensor", CV)
                    if len(cs_evac) >= 2:
                        wait("tensor", CS, cs_evac[-2])
                    bpair = [(ptA, ptB), (ptC, ptD)][j % 2]
                    for w in range(2):
                        bank = bpair[w]
                        op("tensor", lambda e, b=b, w=w, bank=bank: e.matmul(
                            bank[:, 0:128],
                            fq2[b][:, w * 128:w * 128 + 128],
                            s_ids, start=True, stop=True), inc=(CT if w == 1 else None))
                    ct_fq.append(CT.n)
                    wait("scalar", CT)
                    for w in range(2):
                        bank = bpair[w]
                        op("scalar", lambda e, j=j, w=w, bank=bank, fq2T=fq2T:
                           e.activation(
                               out=fq2T[:, w * 512 + j * 128:w * 512 + j * 128 + 128],
                               in_=bank[:, 0:128],
                               func=AF.Identity), inc=(CS if w == 1 else None))
                    cs_evac.append(CS.n)
                ctfq_last.append(ct_fq[-1])
                op("scalar", lambda e, h=h, tq=tq, qh=qh: e.activation(
                    out=qh[0:32, 0:512],
                    in_=qTp[h // 4][32 * (h % 4):32 * (h % 4) + 32,
                                    tq * 512:tq * 512 + 512], func=AF.Identity), inc=CS)
                cs_fq = CS.n
                bo = po16[h % 2]
                if nb >= 2:
                    wait("tensor", CS, ob[nb - 2])
                    wait("tensor", CV, ob_v[nb - 2])
                wait("tensor", CS, cs_fq)
                op("tensor", lambda e, h=h, bo=bo, qh=qh: e.matmul(
                    bo[0:33, :], kvk[h][:, 0:33], qh[0:32, 0:512],
                    start=True, stop=False))
                op("tensor", lambda e, h=h, bo=bo, fq2T=fq2T: e.matmul(
                    bo[0:33, :], kvx2a[h][:, 0:33], fq2T[:, 0:512],
                    start=False, stop=False))
                op("tensor", lambda e, h=h, bo=bo, fq2T=fq2T: e.matmul(
                    bo[0:33, :], kvx2b[h][:, 0:33], fq2T[:, 512:1024],
                    start=False, stop=True), inc=CT)
                ct_den.append(CT.n)
                wait("scalar", CT)
                op("scalar", lambda e, h=h, tq=tq, bo=bo: e.activation(
                    out=outp[h // 4][32 * (h % 4):32 * (h % 4) + 16,
                                     tq * 512:tq * 512 + 512],
                    in_=bo[0:16, :], func=AF.Identity,
                    bias=ccol[h][0:16, 0:1]), inc=CS)
                ob.append(CS.n)
                wait("vector", CT)
                dbuf, dbase = den_slot(h)
                op("vector", lambda e, bo=bo, dbuf=dbuf, dbase=dbase, tq=tq:
                   e.tensor_scalar(
                       out=dbuf[dbase:dbase + 1, tq * 512:tq * 512 + 512],
                       in0=bo[32:33, 0:512], scalar1=2048.0, scalar2=1e-5,
                       op0=ALU.add, op1=ALU.max), inc=CV)
                ob_v.append(CV.n)
                nb += 1
        cs_outp = CS.n
        pfree("po1", "po0", "ptD", "ptC")
        dgath = pvw(73728, [128, 2048], F32)
        drain("vector")
        wait("vector", CS, cs_outp)
        wait("sync", CV, ob_v[-1])
        for h in range(8):
            dbuf, dbase = den_slot(h)
            dma("sync", CD, dgath[h:h + 1, :], dbuf[dbase:dbase + 1, :])
        wait("vector", CD)
        op("vector", lambda e: e.reciprocal(out=dgath[0:8, :], in_=dgath[0:8, :]),
           inc=CV)
        wait("sync", CV)
        for h in range(8):
            dbuf, dbase = den_slot(h)
            dma("sync", CD, dbuf[dbase:dbase + 1, :], dgath[h:h + 1, :])
        wait("tensor", CD)
        pbc = palloc("pbc", [128, 2048])
        po = [palloc("poA", [128, 512]), palloc("poB", [128, 512])]
        band_tt = {}
        for half in range(2):
            for hh in range(4):
                h = half * 4 + hh
                dbuf, dbase = den_slot(h)
                band = 32 * hh if hh < 3 else 0
                if band in band_tt:
                    wait("tensor", CV, band_tt[band])
                for seg in range(4):
                    op("tensor", lambda e, dbuf=dbuf, dbase=dbase, band=band, seg=seg:
                       e.matmul(
                           pbc[band:band + 32, seg * 512:seg * 512 + 512],
                           s_ones[dbase:dbase + 1, 0:32],
                           dbuf[dbase:dbase + 1, seg * 512:seg * 512 + 512],
                           start=True, stop=True), inc=(CT if seg == 3 else None))
                wait("vector", CT)
                op("vector", lambda e, half=half, hh=hh, band=band: e.tensor_tensor(
                    out=outs[half][32 * hh:32 * hh + 32, :],
                    in0=outp[half][32 * hh:32 * hh + 32, :],
                    in1=pbc[band:band + 32, :], op=ALU.mult), inc=CV)
                band_tt[band] = CV.n
        cv_outs = CV.n
        wait("tensor", CV, cv_outs)
        deo = []
        for tc in range(NTC):
            if tc >= 1:
                wait("tensor", CV, deo[tc - 1])
            for c in range(2):
                for hf in range(2):
                    op("tensor", lambda e, tc=tc, c=c, hf=hf: e.matmul(
                        po[hf], outs[c][:, tc * 128:tc * 128 + 128],
                        woutb[:, c * 1024 + hf * 512:c * 1024 + hf * 512 + 512],
                        start=(c == 0), stop=(c == 1)),
                       inc=(CT if (c == 1 and hf == 1) else None))
            wait("vector", CT)
            for hf in range(2):
                op("vector", lambda e, tc=tc, hf=hf: e.tensor_tensor(
                    out=s_x[:, tc * H + hf * 512:tc * H + hf * 512 + 512],
                    in0=s_x[:, tc * H + hf * 512:tc * H + hf * 512 + 512],
                    in1=po[hf], op=ALU.add), inc=(CV if hf == 1 else None))
            deo.append(CV.n)
            drain("vector")
        pfree("poB", "poA", "pbc")
        return dict(outp=outp, outs=outs, qTp=qTp, kvk=kvk, kvx2a=kvx2a, kvx2b=kvx2b,
                    k_tok=k_tok, v_tok=v_tok, qtq=qtq, ccol=ccol,
                    sden=[sdenAB[0], sdenAB[1], sdenC])

    if "1" in PH:
        ffn_pre(0)
        qw_state["cs0"] = CS.n
        cws_gu, cws_all = quant_all()
        ffn(0, pre_done=True)
        if "ff" in DBG:
            dcols = dump_decl("cols", [128, 168])
            dsc = dump_decl("sc", [1, 16])
            dcol8 = dump_decl("col8", [128, 8])
            dxi = dump_decl("xi", [128, 2048])
            dsg0 = dump_decl("sg0", [128, 2048])
            barrier(("sync", "vector"))
            packs = [(s_m16, 16), (s_v16, 16), (s_r16, 16), (s_a16, 16), (s_b16, 16),
                     (s_ymax, 16), (s_deqg, 24), (s_sgmax, 24), (s_swg, 24)]
            dpk = pvw(0, [128, 168], F32)
            off = 0
            for tsrc, wdt in packs:
                op("vector", lambda e, t=tsrc, o=off, w=wdt: e.tensor_copy(
                    out=dpk[:, o:o + w], in_=t), inc=None)
                off += wdt
            op("vector", lambda e: e.tensor_copy(out=s_y[:, 0:2048],
                                                 in_=s_xiT[:, 0:2048]), inc=CV)
            wait("sync", CV)
            dma("sync", CD, dcols, dpk)
            dma("sync", CD, dsc, s_sc)
            dma("sync", CD, dcol8, s_col)
            dma("sync", CD, dxi, s_y[:, 0:2048])
            wait("sync", CD)
            dma("sync", CD, dsg0, d_sg[0:128, :])
            dswdb2 = dump_decl("swdb2", [128, 1024])
            wait("sync", CV)
            dma("sync", CD, dswdb2, s_swdb2)
        if "x1" in DBG:
            dx1 = dump_decl("x1", [T, H])
            barrier(("sync",))
            for tc in range(NTC):
                dma("sync", CD, dx1[128 * tc:128 * (tc + 1), :], s_x[:, tc * H:(tc + 1) * H])
    if "1" not in PH:
        cws_gu, cws_all = quant_all()
        barrier()
    if "2" in PH:
        barrier()
        att_dbg = attention()
        if "at" in DBG:
            drden = dump_decl("rden", [32, 2048])
            doutp0 = dump_decl("outp0", [128, 2048])
            douts0 = dump_decl("outs0", [128, 2048])
            dqtp0 = dump_decl("qtp0", [128, 2048])
            barrier(("sync", "vector"))
            for h in range(8):
                if h % 4 < 3:
                    src = att_dbg["sden"][h // 4][32 * (h % 4):32 * (h % 4) + 1, :]
                else:
                    src = att_dbg["sden"][2][32 * (h // 4):32 * (h // 4) + 1, :]
                dma("sync", CD, drden[h:h + 1, :], src)
            dma("sync", CD, doutp0, att_dbg["outp"][0])
            dma("sync", CD, douts0, att_dbg["outs"][0])
            dma("sync", CD, dqtp0, att_dbg["qTp"][0])
        if "kv" in DBG:
            dktok = dump_decl("ktok", [128, 2048])
            dvtok = dump_decl("vtok", [128, 2048])
            dkvst = dump_decl("kvst", [128, 64])
            barrier(("sync", "vector"))
            kpk = att_dbg["qtq"]
            op("vector", lambda e: e.memset(kpk[:, 0:64], 0.0))
            drain("vector")
            op("vector", lambda e: e.tensor_copy(out=kpk[0:32, 0:16],
                                                 in_=att_dbg["kvk"][0][:, 0:16]))
            op("vector", lambda e: e.tensor_copy(out=kpk[0:32, 16:17],
                                                 in_=att_dbg["kvk"][0][:, 32:33]))
            op("vector", lambda e: e.tensor_copy(out=kpk[:, 18:34],
                                                 in_=att_dbg["kvx2a"][0][:, 0:16]))
            op("vector", lambda e: e.tensor_copy(out=kpk[:, 34:35],
                                                 in_=att_dbg["kvx2a"][0][:, 32:33]))
            op("vector", lambda e: e.tensor_copy(out=kpk[:, 36:52],
                                                 in_=att_dbg["kvx2b"][0][:, 0:16]))
            op("vector", lambda e: e.tensor_copy(out=kpk[:, 52:53],
                                                 in_=att_dbg["kvx2b"][0][:, 32:33]))
            op("vector", lambda e: e.tensor_copy(out=kpk[0:32, 54:58],
                                                 in_=att_dbg["ccol"][0]), inc=CV)
            stg = att_dbg["qTp"][0]
            op("vector", lambda e: e.tensor_copy(out=stg, in_=att_dbg["k_tok"]), inc=CV)
            wait("sync", CV)
            dma("sync", CD, dktok, stg)
            wait("vector", CD)
            op("vector", lambda e: e.tensor_copy(out=stg, in_=att_dbg["v_tok"]), inc=CV)
            wait("sync", CV)
            dma("sync", CD, dvtok, stg)
            dma("sync", CD, dkvst, kpk[:, 0:64])
        if "x2" in DBG:
            dx2 = dump_decl("x2", [T, H])
            barrier(("sync",))
            for tc in range(NTC):
                dma("sync", CD, dx2[128 * tc:128 * (tc + 1), :], s_x[:, tc * H:(tc + 1) * H])
    if "3" in PH:
        wait("tensor", CS)
        wait("tensor", CV)
        wait("sync", CT)
        wait("sync", CV)
        ffn(1)
    else:
        barrier(("sync",))
        for tc in range(NTC):
            dma("sync", CD, d_out[128 * tc:128 * (tc + 1), :], s_x[:, tc * H:(tc + 1) * H])
    barrier(("sync",))
    wait("sync", CD)

    with contextlib.ExitStack() as stack:
        for c in sems:
            c.h = stack.enter_context(nc.semaphore(c.name))
        with nc.Block() as block:
            @block.sync
            def _(e):
                for f in q["sync"]:
                    f(e)

            @block.scalar
            def _(e):
                for f in q["scalar"]:
                    f(e)

            @block.vector
            def _(e):
                for f in q["vector"]:
                    f(e)

            @block.tensor
            def _(e):
                for f in q["tensor"]:
                    f(e)

            @block.gpsimd
            def _(e):
                for f in q["gpsimd"]:
                    f(e)
    return nc, dumps


_CACHE = {}


def kernel(L0_attn_x, L0_attn_gamma, L0_attn_w_q, L0_attn_w_kv, L0_attn_w_out,
           L0_ffn_clip_x, L0_ffn_clip_down, L0_ffn_w_gate, L0_ffn_w_up, L0_ffn_w_down):
    if "nc" not in _CACHE:
        _CACHE["nc"], _CACHE["dumps"] = build_nc()
    nc, dumps = _CACHE["nc"], _CACHE["dumps"]
    x = np.ascontiguousarray(np.asarray(L0_attn_x, dtype=np.float32))
    wq = np.asarray(L0_attn_w_q, np.float32)
    wqp = np.zeros((H, 256), np.float32)
    for h in range(8):
        wqp[:, 32 * h:32 * h + 16] = wq[:, 16 * h:16 * h + 16]
    wo = np.asarray(L0_attn_w_out, np.float32)
    wop = np.zeros((256, H), np.float32)
    for h in range(8):
        wop[32 * h:32 * h + 16, :] = wo[16 * h:16 * h + 16, :]
    common = dict(
        gamma=np.ascontiguousarray(np.asarray(L0_attn_gamma, np.float32)),
        wqp=wqp, wkv=np.ascontiguousarray(np.asarray(L0_attn_w_kv, np.float32)),
        woutp=wop,
        clipx=np.asarray(L0_ffn_clip_x, np.float32).reshape(1, 1),
        clipd=np.asarray(L0_ffn_clip_down, np.float32).reshape(1, 1),
        wg=np.ascontiguousarray(np.asarray(L0_ffn_w_gate, np.float32)),
        wu=np.ascontiguousarray(np.asarray(L0_ffn_w_up, np.float32)),
        wd=np.ascontiguousarray(np.asarray(L0_ffn_w_down, np.float32)),
        idf=np.eye(128, dtype=np.float32),
        ids=(np.eye(128) * np.sqrt(0.5)).astype(np.float32),
    )
    in_maps = [dict(common, x=x[c]) for c in range(8)]
    res = run_bass_kernel_spmd(nc, in_maps, core_ids=list(range(8)))
    kernel.last = res
    return np.stack([res.results[c]["out"] for c in range(8)], axis=0)


# version 16
# speedup vs baseline: 1.0130x; 1.0130x over previous
import os
import contextlib
import numpy as np
import ml_dtypes
import concourse.bass as bass
import concourse.mybir as mybir
from concourse.bass_utils import run_bass_kernel_spmd

F32 = mybir.dt.float32
BF16 = mybir.dt.bfloat16
I8 = mybir.dt.int8
U8 = mybir.dt.uint8
AF = mybir.ActivationFunctionType
ALU = mybir.AluOpType
AX = mybir.AxisListType

T = 2048
NTC = 16
H = 1024
NHC = 8
II = 2816
NIC = 22
EPS = 1e-5
QMAX = 127.0
SQH = float(np.float32(np.sqrt(0.5)))

PH = "123"
DBG = ""


class Ctr:
    _i = 0

    def __init__(self):
        Ctr._i += 1
        self.name = f"c{Ctr._i}"
        self.h = None
        self.n = 0


def build_nc():
    nc = bass.Bass(trn_type="TRN2", target_bir_lowering=False)
    q = {k: [] for k in ("sync", "scalar", "vector", "tensor", "gpsimd")}

    d_x = nc.dram_tensor("x", [T, H], F32, kind="ExternalInput").ap()
    d_gamma = nc.dram_tensor("gamma", [H], F32, kind="ExternalInput").ap()
    d_wqp = nc.dram_tensor("wqp", [H, 256], F32, kind="ExternalInput").ap()
    d_wkv = nc.dram_tensor("wkv", [H, 256], F32, kind="ExternalInput").ap()
    d_woutp = nc.dram_tensor("woutp", [256, H], F32, kind="ExternalInput").ap()
    d_clipx = nc.dram_tensor("clipx", [1, 1], F32, kind="ExternalInput").ap()
    d_clipd = nc.dram_tensor("clipd", [1, 1], F32, kind="ExternalInput").ap()
    d_wg = nc.dram_tensor("wg", [II, H], F32, kind="ExternalInput").ap()
    d_wu = nc.dram_tensor("wu", [II, H], F32, kind="ExternalInput").ap()
    d_wd = nc.dram_tensor("wd", [H, II], F32, kind="ExternalInput").ap()
    d_idf = nc.dram_tensor("idf", [128, 128], F32, kind="ExternalInput").ap()
    d_ids = nc.dram_tensor("ids", [128, 128], F32, kind="ExternalInput").ap()
    d_out = nc.dram_tensor("out", [T, H], F32, kind="ExternalOutput").ap()

    d_wgT = nc.dram_tensor("wgT_s", [II, H], BF16, kind="Internal").ap()
    d_wuT = nc.dram_tensor("wuT_s", [II, H], BF16, kind="Internal").ap()
    d_wdT = nc.dram_tensor("wdT_s", [II, H], BF16, kind="Internal").ap()
    d_sg = nc.dram_tensor("sg_s", [II, T], F32, kind="Internal").ap()
    d_ccin = [nc.dram_tensor(f"ccin{i}", [1, 128], F32, kind="Internal").ap() for i in range(4)]
    d_ccout = [nc.dram_tensor(f"ccout{i}", [8, 128], F32, kind="Internal").ap() for i in range(4)]

    dumps = {}

    def dump_decl(name, shape):
        dumps[name] = nc.dram_tensor("dbg_" + name, shape, F32, kind="ExternalOutput").ap()
        return dumps[name]

    A = nc.alloc_sbuf_tensor
    s_x = A("s_x", [128, NTC * H], F32).ap()
    h_xiT = A("s_xiT", [128, NHC * T], BF16)
    s_xiT = h_xiT.ap()
    s_y = A("s_y", [128, 2 * H], F32).ap()
    s_q8 = A("s_q8", [128, 2 * H], I8).ap()
    s_qb = A("s_qb", [128, 2 * H], BF16).ap()
    s_idf = A("s_idf", [128, 128], F32).ap()
    s_idb = A("s_idb", [128, 128], BF16).ap()
    s_ids = A("s_ids", [128, 128], F32).ap()
    s_gam = A("s_gam", [128, 8], F32).ap()
    s_swg = A("s_swg", [128, 24], F32).ap()
    s_swu = A("s_swu", [128, 24], F32).ap()
    s_swd = A("s_swd", [128, 8], F32).ap()
    s_swdb = A("s_swdb", [128, H], F32).ap()
    s_swdb2 = A("s_swdb2", [128, H], F32).ap()
    s_stats = A("s_stats", [128, 192], F32).ap()
    s_msq = A("s_msq", [128, 64], F32).ap()
    s_m16 = A("s_m16", [128, 16], F32).ap()
    s_v16 = A("s_v16", [128, 16], F32).ap()
    s_r16 = A("s_r16", [128, 16], F32).ap()
    s_a16 = A("s_a16", [128, 16], F32).ap()
    s_b16 = A("s_b16", [128, 16], F32).ap()
    s_t16a = A("s_t16a", [128, 16], F32).ap()
    s_t16b = A("s_t16b", [128, 16], F32).ap()
    s_t16c = A("s_t16c", [128, 16], F32).ap()
    s_ymax = A("s_ymax", [128, 16], F32).ap()
    s_sgmax = A("s_sgmax", [128, 24], F32).ap()
    s_red1 = A("s_red1", [128, 2], F32).ap()
    s_row128 = A("s_row128", [1, 128], F32).ap()
    s_ones = A("s_ones", [128, 128], F32).ap()
    s_sel4 = A("s_sel4", [1, 512], F32).ap()
    s_g8 = A("s_g8", [8, 128], F32).ap()
    s_red8 = A("s_red8", [8, 2], F32).ap()
    s_sc = A("s_sc", [1, 16], F32).ap()
    s_sig = A("s_sig", [1, 4], F32).ap()
    s_col = A("s_col", [128, 8], F32).ap()
    s_deqg = A("s_deqg", [128, 24], F32).ap()
    s_dequ = A("s_dequ", [128, 24], F32).ap()
    s_rden = A("s_rden", [128, 128], F32).ap()
    s_cst = A("s_cst", [128, 4], F32).ap()
    POOL = 83968
    h_pool = A("s_pool", [128, POOL], U8)
    pool_addr = nc.lookup_mloc(h_pool).addr
    xi_addr = nc.lookup_mloc(h_xiT).addr
    _n = [0]

    def _at(base, off, shape, dtype):
        _n[0] += 1
        return nc.alloc_sbuf_tensor_at(f"v{_n[0]}", shape, dtype, offset=base + off).ap()

    def pvw(off, shape, dtype):
        assert off + int(np.prod(shape[1:])) * mybir.dt.size(dtype) <= POOL, (off, shape)
        return _at(pool_addr, off, shape, dtype)

    def xvw(off, shape, dtype):
        assert off + int(np.prod(shape[1:])) * mybir.dt.size(dtype) <= 2 * NHC * T, (off, shape)
        return _at(xi_addr, off, shape, dtype)

    guards = {}

    _pn = [0]

    def palloc(name, shape):
        _pn[0] += 1
        g = nc.psum_tensor(f"ps_{name}_{_pn[0]}", shape, F32)
        t = g.__enter__()
        guards[name] = g
        return t.ap()

    def pfree(*names):
        for n in names:
            guards.pop(n).__exit__(None, None, None)

    ptA = palloc("ptA", [128, 512])
    ptB = palloc("ptB", [128, 512])

    sems = []

    def ctr():
        c = Ctr()
        sems.append(c)
        return c

    CD, CWS, CWL, CWD, CSP, CSR, CG, CC, CV, CS, CT = (ctr() for _ in range(11))
    CSP3 = [ctr() for _ in range(3)]
    COD2 = [ctr() for _ in range(2)]
    CWL2 = [ctr() for _ in range(2)]
    CSR4 = [ctr() for _ in range(4)]
    CWLP = [ctr() for _ in range(4)]
    CWS2 = [ctr() for _ in range(2)]

    def dma(eng, c, out, in_, slow=False):
        c.n += 16
        q[eng].append(lambda e, o=out, i=in_, cc=c, s=slow: e.dma_start(
            out=o, in_=i, allow_slow_non_contiguous=s).then_inc(cc.h, 16))

    def wait(eng, c, val=None):
        v = c.n if val is None else val
        if v > 0:
            q[eng].append(lambda e, cc=c, vv=v: e.wait_ge(cc.h, vv))

    def op(eng, fn, inc=None):
        if inc is None:
            q[eng].append(fn)
        else:
            inc.n += 1
            q[eng].append(lambda e, f=fn, cc=inc: f(e).then_inc(cc.h, 1))

    def drain(eng):
        q[eng].append(lambda e: e.drain())

    def barrier(engines=("sync", "scalar", "vector", "tensor", "gpsimd")):
        snap = [(c, c.n) for c in sems if c.n > 0]
        for eng in engines:
            for c, v in snap:
                wait(eng, c, v)

    def dr3(ap, p=128):
        return ap.rearrange("(c p) h -> c p h", p=p).transpose([1, 0, 2])

    def sb3(ap, h):
        return ap.rearrange("p (c h) -> p c h", h=h)

    dma("sync", CD, sb3(s_x, H), dr3(d_x))
    dma("sync", CD, s_idf, d_idf)
    dma("sync", CD, s_ids, d_ids)
    dma("sync", CD, s_gam, d_gamma.rearrange("(c p) -> c p", p=128).transpose([1, 0]), slow=True)
    dma("sync", CD, s_sig[0:1, 0:1], d_clipx)
    dma("sync", CD, s_sig[0:1, 2:3], d_clipd)
    boot_cd = CD.n

    wait("vector", CD, boot_cd)
    op("vector", lambda e: e.memset(s_ones, 1.0))
    op("vector", lambda e: e.memset(s_swg, 0.0))
    op("vector", lambda e: e.memset(s_swu, 0.0))
    op("vector", lambda e: e.memset(s_sgmax, 0.0))
    op("vector", lambda e: e.memset(s_sc, 0.0))
    op("vector", lambda e: e.memset(s_col, 0.0))
    op("vector", lambda e: e.memset(s_cst[:, 0:1], EPS))
    op("vector", lambda e: e.memset(s_cst[:, 1:2], float(H)))
    op("vector", lambda e: e.memset(s_cst[:, 2:3], 0.25))
    op("vector", lambda e: e.memset(s_cst[:, 3:4], SQH))
    op("vector", lambda e: e.memset(s_sel4, 0.0))
    drain("vector")
    for j in range(4):
        op("vector", lambda e, j=j: e.memset(
            s_sel4[0:1, 128 * j + 32 * j:128 * j + 32 * j + 32], 1.0))
    op("vector", lambda e: e.tensor_copy(out=s_idb, in_=s_idf), inc=CV)
    cv_idb = CV.n

    wait("scalar", CD, boot_cd)
    op("scalar", lambda e: e.activation(out=s_sig[0:1, 1:2], in_=s_sig[0:1, 0:1], func=AF.Sigmoid))
    op("scalar", lambda e: e.activation(out=s_sig[0:1, 3:4], in_=s_sig[0:1, 2:3], func=AF.Sigmoid),
       inc=CS)

    p0_wraw = [pvw(0, [128, 2816], F32), pvw(11264, [128, 2816], F32),
               pvw(22528, [128, 2816], F32), pvw(33792, [128, 2816], F32)]
    p0_w8 = [pvw(45056, [128, 2816], I8), pvw(47872, [128, 2816], I8)]
    p0_wqb = [pvw(50688, [128, 2816], BF16), pvw(56320, [128, 2816], BF16)]
    p0_stg = [pvw(61952, [128, 2816], BF16), pvw(67584, [128, 2816], BF16)]

    qw_state = {}
    ccst = {}

    def quant_all():
        if "row_cv" in ccst:
            wait("tensor", CV, ccst["row_cv"])
        chunks = []
        for d_w, d_wT, s_scales, nrc, ncol, rm in (
                (d_wg, d_wgT, s_swg, NIC, H, True),
                (d_wu, d_wuT, s_swu, NIC, H, True),
                (d_wd, d_wdT, s_swd, NHC, II, False)):
            for cl in range(nrc):
                chunks.append((d_w, d_wT, s_scales, cl, ncol, rm))
        n = len(chunks)
        cs_done, ct_done, sp_done, cv_recip = {}, {}, {}, {}

        def scale_part(ci):
            d_w, d_wT, s_scales, cl, ncol, rm = chunks[ci]
            rb4 = ci % 4
            buf = p0_wraw[rb4]
            sl = ci % 2
            if ci >= 4:
                wait("sync", CS, cs_done[ci - 4])
            dma("sync", CWLP[rb4], buf[:, 0:ncol],
                d_w[128 * cl:128 * (cl + 1), :])
            wait("vector", CWLP[rb4])
            op("vector", lambda e, cl=cl, buf=buf, s_scales=s_scales, ncol=ncol:
               e.tensor_reduce(out=s_scales[:, cl:cl + 1],
                               in_=buf[:, 0:ncol], axis=AX.X,
                               op=ALU.max, apply_absolute_value=True))
            drain("vector")
            if ci >= 2:
                wait("vector", CS, cs_done[ci - 2])
            elif qw_state.get("cs0"):
                wait("vector", CS, qw_state["cs0"])
            op("vector", lambda e, cl=cl, sl=sl, s_scales=s_scales: e.tensor_scalar(
                out=s_t16a[:, sl:sl + 1], in0=s_scales[:, cl:cl + 1],
                scalar1=1.0 / QMAX, scalar2=None, op0=ALU.mult))
            drain("vector")
            op("vector", lambda e, sl=sl: e.reciprocal(
                out=s_t16b[:, sl:sl + 1], in_=s_t16a[:, sl:sl + 1]), inc=CV)
            cv_recip[ci] = CV.n

        def process_part(ci):
            d_w, d_wT, s_scales, cl, ncol, rm = chunks[ci]
            rb4 = ci % 4
            buf = p0_wraw[rb4]
            b = ci % 2
            sl = ci % 2
            w8, qb, stg = p0_w8[b], p0_wqb[b], p0_stg[b]
            ncc = ncol // 128
            wait("scalar", CV, cv_recip[ci])
            if ci >= 2:
                wait("scalar", CT, ct_done[ci - 2])
            op("scalar", lambda e, buf=buf, w8=w8, ncol=ncol, sl=sl: e.activation(
                out=w8[:, 0:ncol], in_=buf[:, 0:ncol],
                func=AF.Identity, scale=s_t16b[:, sl:sl + 1]))
            drain("scalar")
            op("scalar", lambda e, qb=qb, w8=w8, ncol=ncol: e.activation(
                out=qb[:, 0:ncol], in_=w8[:, 0:ncol], func=AF.Identity), inc=CS)
            cs_done[ci] = CS.n
            wait("tensor", CS)
            if ci == 0:
                wait("tensor", CV, cv_idb)
            if ci >= 2:
                wait("vector", CWS2[b], sp_done[ci - 2])
            ngrp = (ncc + 3) // 4
            for g in range(ngrp):
                bank = ptA if g % 2 == 0 else ptB
                key = ("pgcv", g % 2)
                if key in qw_state:
                    wait("tensor", CV, qw_state[key])
                tiles = list(range(4 * g, min(4 * g + 4, ncc)))
                for jj, cc in enumerate(tiles):
                    op("tensor", lambda e, cc=cc, qb=qb, bank=bank, jj=jj:
                       e.matmul(bank[:, jj * 128:jj * 128 + 128],
                                qb[:, cc * 128:cc * 128 + 128],
                                s_idb, start=True, stop=True),
                       inc=(CT if jj == len(tiles) - 1 else None))
                wait("vector", CT)
                w = len(tiles) * 128
                op("vector", lambda e, g=g, stg=stg, bank=bank, w=w: e.tensor_copy(
                    out=stg[:, g * 512:g * 512 + w], in_=bank[:, 0:w]), inc=CV)
                qw_state[key] = CV.n
            ct_done[ci] = CT.n
            wait("gpsimd", CV)
            if rm:
                dma("gpsimd", CWS2[b],
                    d_wT[128 * cl:128 * (cl + 1), 0:ncc * 128],
                    stg[:, 0:ncc * 128])
            else:
                for cc in range(ncc):
                    dma("gpsimd", CWS2[b],
                        d_wT[128 * cc:128 * (cc + 1), 128 * cl:128 * cl + 128],
                        stg[:, cc * 128:cc * 128 + 128])
            sp_done[ci] = CWS2[b].n

        for ci in range(n):
            scale_part(ci)
            if ci >= 1:
                process_part(ci - 1)
        process_part(n - 1)
        gu = (CWS2[0].n, CWS2[1].n)
        allw = gu
        if "ccin" in ccst:
            wait("vector", CG, ccst["ccin"])
        wait("tensor", CV)
        for c in range(NHC):
            op("tensor", lambda e, c=c: e.matmul(ptA[0:1, 0:128], s_swd[:, c:c + 1],
                                                 s_idf, start=True, stop=True), inc=CT)
            wait("vector", CT)
            op("vector", lambda e: e.tensor_copy(out=s_row128, in_=ptA[0:1, 0:128]),
               inc=CV)
            wait("tensor", CV)
            op("tensor", lambda e: e.matmul(ptB[:, 0:128], s_ones[0:1, :], s_row128,
                                            start=True, stop=True), inc=CT)
            wait("vector", CT)
            op("vector", lambda e, c=c: e.tensor_copy(
                out=s_swdb[:, c * 128:(c + 1) * 128], in_=ptB[:, 0:128]), inc=CV)
            wait("tensor", CV)
        return gu, allw

    def ln_stats():
        for tc in range(NTC):
            xc = s_x[:, tc * H:(tc + 1) * H]
            op("vector", lambda e, xc=xc, tc=tc: e.bn_stats(
                out=s_stats[:, tc * 12:tc * 12 + 6], in_=xc[:, 0:512]))
            op("vector", lambda e, xc=xc, tc=tc: e.bn_stats(
                out=s_stats[:, tc * 12 + 6:tc * 12 + 12], in_=xc[:, 512:1024]))
        drain("vector")
        v_m = s_stats.rearrange("p (t g c) -> p t g c", g=4, c=3)[:, :, :, 1:2] \
                     .rearrange("p t g o -> p t (g o)")
        v_M2 = s_stats.rearrange("p (t g c) -> p t g c", g=4, c=3)[:, :, :, 2:3] \
                      .rearrange("p t g o -> p t (g o)")
        v_mflat = s_stats.rearrange("p (a c) -> p a c", c=3)[:, :, 1:2] \
                         .rearrange("p a o -> p (a o)")
        op("vector", lambda e: e.tensor_reduce(out=s_m16, in_=v_m, axis=AX.X, op=ALU.add))
        op("vector", lambda e: e.tensor_reduce(out=s_v16, in_=v_M2, axis=AX.X, op=ALU.add))
        op("vector", lambda e: e.tensor_tensor(out=s_msq, in0=v_mflat, in1=v_mflat, op=ALU.mult))
        drain("vector")
        v_msq = s_msq.rearrange("p (t g) -> p t g", g=4)
        op("vector", lambda e: e.tensor_reduce(out=s_t16a, in_=v_msq, axis=AX.X, op=ALU.add))
        op("vector", lambda e: e.tensor_scalar(out=s_m16, in0=s_m16, scalar1=0.25,
                                               scalar2=None, op0=ALU.mult))
        op("vector", lambda e: e.tensor_scalar(out=s_v16, in0=s_v16, scalar1=1.0 / H,
                                               scalar2=None, op0=ALU.mult))
        drain("vector")
        op("vector", lambda e: e.tensor_scalar(out=s_t16b, in0=s_t16a, scalar1=0.25,
                                               scalar2=None, op0=ALU.mult))
        op("vector", lambda e: e.tensor_tensor(out=s_t16c, in0=s_m16, in1=s_m16, op=ALU.mult))
        drain("vector")
        op("vector", lambda e: e.tensor_tensor(out=s_v16, in0=s_v16, in1=s_t16b, op=ALU.add))
        drain("vector")
        op("vector", lambda e: e.tensor_tensor(out=s_v16, in0=s_v16, in1=s_t16c, op=ALU.subtract),
           inc=CV)
        drain("vector")

    def cross_core_max_pre(src, ncols, ag_idx):
        drain("vector")
        op("vector", lambda e, src=src, ncols=ncols: e.tensor_reduce(
            out=s_red1[:, 0:1], in_=src[:, 0:ncols], axis=AX.X, op=ALU.max), inc=CV)
        wait("tensor", CV)
        op("tensor", lambda e: e.matmul(ptA[0:1, 0:128], s_red1[:, 0:1], s_idf,
                                        start=True, stop=True), inc=CT)
        wait("vector", CT)
        op("vector", lambda e: e.tensor_copy(out=s_row128, in_=ptA[0:1, 0:128]))
        drain("vector")
        op("vector", lambda e: e.tensor_reduce(out=s_sc[0:1, 14:15], in_=s_row128,
                                               axis=AX.X, op=ALU.max))
        drain("vector")
        op("vector", lambda e: e.tensor_copy(
            out=s_row128, in_=s_sc[0:1, 14:15].broadcast_to([1, 128])), inc=CV)
        ccst["row_cv"] = CV.n
        wait("gpsimd", CV)
        dma("gpsimd", CG, d_ccin[ag_idx], s_row128)
        ccst["ccin"] = CG.n
        wait("gpsimd", CG)
        op("gpsimd", lambda e, i=ag_idx: e.collective_compute(
            "AllGather", ALU.bypass, replica_groups=[list(range(8))],
            ins=[d_ccin[i].opt()], outs=[d_ccout[i].opt()]), inc=CC)
        wait("gpsimd", CC)
        dma("gpsimd", CG, s_g8, d_ccout[ag_idx])

    def cross_core_max_post(out_slot):
        wait("vector", CG)
        op("vector", lambda e: e.tensor_reduce(out=s_red8[:, 0:1], in_=s_g8,
                                               axis=AX.X, op=ALU.max), inc=CV)
        wait("tensor", CV)
        op("tensor", lambda e: e.matmul(ptA[0:1, 0:8], s_red8[0:8, 0:1], s_idf[0:8, 0:8],
                                        start=True, stop=True), inc=CT)
        wait("vector", CT)
        op("vector", lambda e: e.tensor_copy(out=s_row128[0:1, 0:8], in_=ptA[0:1, 0:8]))
        drain("vector")
        op("vector", lambda e, o=out_slot: e.tensor_reduce(
            out=s_sc[0:1, o:o + 1], in_=s_row128[0:1, 0:8], axis=AX.X, op=ALU.max), inc=CV)
        drain("vector")

    def cross_core_max(src, ncols, ag_idx, out_slot):
        cross_core_max_pre(src, ncols, ag_idx)
        cross_core_max_post(out_slot)

    def bcast_scalar(src_slot, dst_col):
        wait("tensor", CV)
        op("tensor", lambda e, s=src_slot: e.matmul(
            ptB[:, 0:1], s_ones[0:1, :], s_sc[0:1, s:s + 1], start=True, stop=True), inc=CT)
        wait("vector", CT)
        op("vector", lambda e, d=dst_col: e.tensor_copy(
            out=s_col[:, d:d + 1], in_=ptB[:, 0:1]), inc=CV)
        drain("vector")
        wait("tensor", CV)

    def ffn_pre(call):
        ag0 = 0 if call == 0 else 2
        ln_stats()
        wait("scalar", CV)
        op("scalar", lambda e: e.activation(out=s_t16b, in_=s_v16, func=AF.Sqrt,
                                            bias=s_cst[:, 0:1]), inc=CS)
        wait("vector", CS)
        op("vector", lambda e: e.reciprocal(out=s_r16, in_=s_t16b))
        drain("vector")
        op("vector", lambda e: e.tensor_tensor(out=s_t16a, in0=s_m16, in1=s_r16,
                                               op=ALU.mult))
        drain("vector")
        op("vector", lambda e: e.tensor_scalar(out=s_b16, in0=s_t16a, scalar1=-1.0,
                                               scalar2=None, op0=ALU.mult), inc=CV)
        wait("scalar", CV)
        yred = []
        for tc in range(NTC):
            pp = tc % 2
            if tc >= 2:
                wait("scalar", CV, yred[tc - 2])
            op("scalar", lambda e, tc=tc, pp=pp: e.activation(
                out=s_y[:, pp * H:(pp + 1) * H], in_=s_x[:, tc * H:(tc + 1) * H],
                func=AF.Identity, scale=s_r16[:, tc:tc + 1],
                bias=s_b16[:, tc:tc + 1]), inc=CS)
            wait("vector", CS)
            op("vector", lambda e, tc=tc, pp=pp: e.tensor_reduce(
                out=s_ymax[:, tc:tc + 1], in_=s_y[:, pp * H:(pp + 1) * H],
                axis=AX.X, op=ALU.max, apply_absolute_value=True), inc=CV)
            yred.append(CV.n)
        drain("vector")
        cross_core_max_pre(s_ymax, 16, ag0)

    def ffn(call, pre_done=False):
        ag0 = 0 if call == 0 else 2
        wgbuf = [pvw(0, [128, 2048], BF16), pvw(4096, [128, 2048], BF16)]
        wubuf = [pvw(8192, [128, 2048], BF16), pvw(12288, [128, 2048], BF16)]
        sgt = [pvw(16384 + 8192 * k, [128, 2048], F32) for k in range(3)]
        slb = [pvw(40960 + 2048 * k, [128, 512], F32) for k in range(2)]
        wdTb = pvw(0, [128, NIC * 1024], BF16)
        hintb = [pvw(45056, [128, NIC * 512], BF16), xvw(0, [128, NIC * 512], BF16)]
        sgrl = [pvw(67584 + 2048 * k, [128, 512], F32) for k in range(4)]
        tdq = [pvw(75776 + 4096 * k, [128, 1024], F32) for k in range(2)]

        if not pre_done:
            ffn_pre(call)
        cross_core_max_post(0)
        op("vector", lambda e: e.tensor_tensor(out=s_sc[0:1, 1:2], in0=s_sc[0:1, 0:1],
                                               in1=s_sig[0:1, 1:2], op=ALU.mult))
        drain("vector")
        op("vector", lambda e: e.tensor_scalar(out=s_sc[0:1, 2:3], in0=s_sc[0:1, 1:2],
                                               scalar1=1.0 / QMAX, scalar2=None, op0=ALU.mult))
        drain("vector")
        op("vector", lambda e: e.reciprocal(out=s_sc[0:1, 3:4], in_=s_sc[0:1, 2:3]), inc=CV)
        drain("vector")
        bcast_scalar(3, 0)
        bcast_scalar(2, 1)
        op("vector", lambda e: e.tensor_scalar(out=s_msq[:, 0:24], in0=s_swg,
                                               scalar1=s_col[:, 1:2], scalar2=None, op0=ALU.mult))
        op("vector", lambda e: e.tensor_scalar(out=s_msq[:, 24:48], in0=s_swu,
                                               scalar1=s_col[:, 1:2], scalar2=None, op0=ALU.mult))
        drain("vector")
        op("vector", lambda e: e.tensor_scalar(out=s_deqg, in0=s_msq[:, 0:24],
                                               scalar1=1.0 / QMAX, scalar2=None, op0=ALU.mult))
        op("vector", lambda e: e.tensor_scalar(out=s_dequ, in0=s_msq[:, 24:48],
                                               scalar1=1.0 / QMAX, scalar2=None, op0=ALU.mult))
        op("vector", lambda e: e.tensor_scalar(out=s_a16, in0=s_r16, scalar1=s_col[:, 0:1],
                                               scalar2=None, op0=ALU.mult))
        drain("vector")
        op("vector", lambda e: e.tensor_tensor(out=s_t16a, in0=s_m16, in1=s_a16, op=ALU.mult))
        drain("vector")
        op("vector", lambda e: e.tensor_scalar(out=s_b16, in0=s_t16a, scalar1=-1.0,
                                               scalar2=None, op0=ALU.mult), inc=CV)
        wait("scalar", CV)
        ct_tp = []
        st4 = {0: CV.n, 1: CV.n}
        for tc in range(NTC):
            pp = tc % 2
            op("scalar", lambda e, tc=tc, pp=pp: e.activation(
                out=s_q8[:, pp * H:(pp + 1) * H], in_=s_x[:, tc * H:(tc + 1) * H],
                func=AF.Identity, scale=s_a16[:, tc:tc + 1], bias=s_b16[:, tc:tc + 1]))
            drain("scalar")
            if tc >= 2:
                wait("scalar", CT, ct_tp[tc - 2])
            op("scalar", lambda e, pp=pp: e.activation(
                out=s_qb[:, pp * H:(pp + 1) * H], in_=s_q8[:, pp * H:(pp + 1) * H],
                func=AF.Identity), inc=CS)
            wait("tensor", CS)
            for g in range(2):
                bank = ptA if g == 0 else ptB
                wait("tensor", CV, st4[g])
                for j in range(4):
                    k = g * 4 + j
                    op("tensor", lambda e, k=k, pp=pp, bank=bank, j=j: e.matmul(
                        bank[:, j * 128:j * 128 + 128],
                        s_qb[:, pp * H + k * 128:pp * H + k * 128 + 128],
                        s_idb, start=True, stop=True), inc=(CT if j == 3 else None))
                wait("vector", CT)
                for j in range(4):
                    k = g * 4 + j
                    op("vector", lambda e, k=k, tc=tc, bank=bank, j=j: e.tensor_copy(
                        out=s_xiT[:, k * T + tc * 128:k * T + tc * 128 + 128],
                        in_=bank[:, j * 128:j * 128 + 128]), inc=(CV if j == 3 else None))
                st4[g] = CV.n
            ct_tp.append(CT.n)
        cv_xiT = CV.n
        if call == 0:
            wait("sync", CWS2[0], cws_all[0])
            wait("sync", CWS2[1], cws_all[1])
        pg = [palloc("pg0", [128, 512]), palloc("pg1", [128, 512])]
        pu = [palloc("pu0", [128, 512]), palloc("pu1", [128, 512])]
        cwl_piece, ct_piece = [], []

        def piece_dma(pc):
            b = pc % 2
            dma("sync", CWL2[b], sb3(wgbuf[b], 1024), dr3(d_wgT[256 * pc:256 * (pc + 1), :]))
            dma("sync", CWL2[b], sb3(wubuf[b], 1024), dr3(d_wuT[256 * pc:256 * (pc + 1), :]))
            cwl_piece.append(CWL2[b].n)

        piece_dma(0)
        wait("tensor", CV, cv_xiT)
        ge, spl, sgred = [], [], []
        for i in range(NIC):
            pc, sub = i // 2, i % 2
            if sub == 0:
                if pc + 1 < 11:
                    if pc >= 1:
                        wait("sync", CT, ct_piece[pc - 1])
                    piece_dma(pc + 1)
                wait("tensor", CWL2[pc % 2], cwl_piece[pc])
            if i >= 3:
                wait("vector", CSP3[i % 3], spl[i - 3])
            for t4 in range(4):
                n = i * 4 + t4
                bg, bu = pg[t4 % 2], pu[t4 % 2]
                if n >= 2:
                    wait("tensor", CV, ge[n - 2])
                for k in range(NHC):
                    op("tensor", lambda e, k=k, sub=sub, b=pc % 2, t4=t4, bg=bg: e.matmul(
                        bg, wgbuf[b][:, sub * 1024 + k * 128:sub * 1024 + k * 128 + 128],
                        s_xiT[:, k * T + t4 * 512:k * T + t4 * 512 + 512],
                        start=(k == 0), stop=(k == NHC - 1)),
                       inc=(CT if k == NHC - 1 else None))
                wait("scalar", CT)
                if n >= 2:
                    wait("scalar", CV, ge[n - 2])
                op("scalar", lambda e, i=i, t4=t4, bg=bg: e.activation(
                    out=slb[t4 % 2], in_=bg, func=AF.Silu, scale=s_deqg[:, i:i + 1]), inc=CS)
                for k in range(NHC):
                    op("tensor", lambda e, k=k, sub=sub, b=pc % 2, t4=t4, bu=bu: e.matmul(
                        bu, wubuf[b][:, sub * 1024 + k * 128:sub * 1024 + k * 128 + 128],
                        s_xiT[:, k * T + t4 * 512:k * T + t4 * 512 + 512],
                        start=(k == 0), stop=(k == NHC - 1)),
                       inc=(CT if k == NHC - 1 else None))
                wait("vector", CT)
                wait("vector", CS)
                op("vector", lambda e, i=i, t4=t4, bu=bu: e.scalar_tensor_tensor(
                    out=sgt[i % 3][:, t4 * 512:t4 * 512 + 512], in0=bu,
                    scalar=s_dequ[:, i:i + 1], in1=slb[t4 % 2],
                    op0=ALU.mult, op1=ALU.mult), inc=CV)
                ge.append(CV.n)
            ct_piece_val = CT.n
            if sub == 1 or i == NIC - 1:
                ct_piece.append(ct_piece_val)
            drain("vector")
            op("vector", lambda e, i=i: e.tensor_reduce(
                out=s_sgmax[:, i:i + 1], in_=sgt[i % 3], axis=AX.X,
                op=ALU.max, apply_absolute_value=True), inc=CV)
            sgred.append(CV.n)
            wait("sync", CV)
            dma("sync", CSP3[i % 3], d_sg[128 * i:128 * (i + 1), :], sgt[i % 3])
            spl.append(CSP3[i % 3].n)
        ct_gate = CT.n
        pfree("pu1", "pu0", "pg1", "pg0")
        wait("sync", CT, ct_gate)
        for c3 in CSP3:
            wait("sync", c3)
        wait("sync", CV)
        if call == 0:
            wait("sync", CWS2[0], cws_all[0])
            wait("sync", CWS2[1], cws_all[1])
        wd_batch = []
        for k4 in range(4):
            lo, hi = 6 * k4, min(6 * k4 + 6, NIC)
            for i in range(lo, hi):
                dma("sync", CWLP[k4], wdTb[:, 1024 * i:1024 * (i + 1)],
                    d_wdT[128 * i:128 * (i + 1), :])
            wd_batch.append(CWLP[k4].n)
        cross_core_max(s_sgmax, 22, ag0 + 1, 4)
        op("vector", lambda e: e.tensor_tensor(out=s_sc[0:1, 5:6], in0=s_sc[0:1, 4:5],
                                               in1=s_sig[0:1, 3:4], op=ALU.mult))
        drain("vector")
        op("vector", lambda e: e.tensor_scalar(out=s_sc[0:1, 6:7], in0=s_sc[0:1, 5:6],
                                               scalar1=1.0 / QMAX, scalar2=None, op0=ALU.mult))
        drain("vector")
        op("vector", lambda e: e.reciprocal(out=s_sc[0:1, 7:8], in_=s_sc[0:1, 6:7]))
        drain("vector")
        op("vector", lambda e: e.tensor_scalar(out=s_sc[0:1, 8:9], in0=s_sc[0:1, 6:7],
                                               scalar1=0.5 / QMAX, scalar2=None, op0=ALU.mult), inc=CV)
        drain("vector")
        bcast_scalar(7, 2)
        bcast_scalar(8, 3)
        op("vector", lambda e: e.tensor_scalar(out=s_swdb2, in0=s_swdb, scalar1=s_col[:, 3:4],
                                               scalar2=None, op0=ALU.mult), inc=CV)
        cv_swdb2 = CV.n
        pd = [palloc("pd0", [128, 1024]), palloc("pd1", [128, 1024])]
        de, outd = [], []
        ct_down = []
        hq_cs, hq_cv = [], []
        wait("scalar", CV, cv_swdb2)
        for t4 in range(4):
            hb = hintb[t4 % 2]
            if t4 >= 2:
                wait("vector", CT, ct_down[t4 - 2])
            for i in range(NIC):
                gi = t4 * NIC + i
                rb = sgrl[gi % 4]
                if gi >= 4:
                    wait("sync", CS, hq_cs[gi - 4])
                dma("sync", CSR4[gi % 4], rb,
                    d_sg[128 * i:128 * (i + 1), t4 * 512:t4 * 512 + 512])
                wait("scalar", CSR4[gi % 4])
                if gi >= 4:
                    wait("scalar", CV, hq_cv[gi - 4])
                op("scalar", lambda e, rb=rb, pp=gi % 4: e.activation(
                    out=s_q8[:, pp * 512:pp * 512 + 512], in_=rb, func=AF.Identity,
                    scale=s_col[:, 2:3]), inc=CS)
                hq_cs.append(CS.n)
                wait("vector", CS)
                op("vector", lambda e, i=i, pp=gi % 4, hb=hb: e.tensor_copy(
                    out=hb[:, i * 512:(i + 1) * 512],
                    in_=s_q8[:, pp * 512:pp * 512 + 512]), inc=CV)
                hq_cv.append(CV.n)
            for m in range(4):
                tc = t4 * 4 + m
                n = t4 * 4 + m
                bank = pd[m % 2]
                if n >= 2:
                    wait("tensor", CV, de[n - 2])
                for i in range(NIC):
                    if m == 0:
                        wait("tensor", CV, hq_cv[t4 * NIC + i])
                    if t4 == 0 and m == 0:
                        wait("tensor", CWLP[i // 6], wd_batch[i // 6])
                    for half in range(2):
                        op("tensor", lambda e, i=i, m=m, half=half, bank=bank, hb=hb: e.matmul(
                            bank[:, half * 512:half * 512 + 512],
                            hb[:, i * 512 + m * 128:i * 512 + m * 128 + 128],
                            wdTb[:, i * 1024 + half * 512:i * 1024 + half * 512 + 512],
                            start=(i == 0), stop=(i == NIC - 1)),
                           inc=(CT if (i == NIC - 1 and half == 1) else None))
                wait("vector", CT)
                wait("vector", CV, cv_swdb2)
                op("vector", lambda e, m=m, bank=bank: e.tensor_tensor(
                    out=tdq[m % 2], in0=bank, in1=s_swdb2, op=ALU.mult))
                drain("vector")
                if call == 0:
                    op("vector", lambda e, tc=tc, m=m: e.tensor_tensor(
                        out=s_x[:, tc * H:(tc + 1) * H], in0=s_x[:, tc * H:(tc + 1) * H],
                        in1=tdq[m % 2], op=ALU.add), inc=CV)
                else:
                    if n >= 2:
                        wait("vector", COD2[m % 2], outd[n - 2])
                    op("vector", lambda e, tc=tc, m=m: e.tensor_tensor(
                        out=s_y[:, (m % 2) * H:(m % 2 + 1) * H],
                        in0=s_x[:, tc * H:(tc + 1) * H],
                        in1=tdq[m % 2], op=ALU.add), inc=CV)
                    wait("sync", CV)
                    dma("sync", COD2[m % 2], d_out[128 * tc:128 * (tc + 1), :],
                        s_y[:, (m % 2) * H:(m % 2 + 1) * H])
                    outd.append(COD2[m % 2].n)
                de.append(CV.n)
                drain("vector")
            ct_down.append(CT.n)
        pfree("pd1", "pd0")

    def attention():
        wqpb = pvw(0, [128, NHC * 256], F32)
        wkvb = pvw(8192, [128, NHC * 256], F32)
        woutb = pvw(16384, [128, 2 * H], F32)
        normT = pvw(24576, [128, NHC * 512], F32)
        qTp = [pvw(40960, [128, 2048], F32), pvw(49152, [128, 2048], F32)]
        outp = [pvw(57344, [128, 2048], F32), pvw(65536, [128, 2048], F32)]
        qtq = pvw(73728, [128, 1024], F32)
        fq2 = [pvw(77824, [128, 256], F32), pvw(78848, [128, 256], F32)]
        fq2Tb = [pvw(79872, [128, 1024], F32), xvw(8192, [128, 1024], F32)]
        qhb = [xvw(20928, [32, 512], F32), xvw(12288, [32, 512], F32)]
        sdenAB = [pvw(24576, [128, 2048], F32), s_y[:, 0:2048]]
        sdenC = xvw(0, [128, 2048], F32)

        def den_slot(h):
            if h % 4 < 3:
                return sdenAB[h // 4], 32 * (h % 4)
            return sdenC, 32 * (h // 4)
        outs = [pvw(32768, [128, 2048], F32), pvw(75776, [128, 2048], F32)]
        k_tok = xvw(0, [128, 2048], F32)
        v_tok = xvw(8192, [128, 2048], F32)
        kvstg = xvw(16384, [128, 512], F32)
        fkb = [xvw(18432, [128, 274], F32), xvw(19584, [128, 274], F32),
               xvw(27072, [128, 274], F32), xvw(28224, [128, 274], F32)]
        vpb = [xvw(20736, [128, 18], F32), xvw(20832, [128, 18], F32),
               xvw(29376, [128, 18], F32), xvw(29472, [128, 18], F32)]
        kvk = [xvw(22976 + 160 * h, [32, 34], F32) for h in range(8)]
        kvx2a = [xvw(24256 + 160 * h, [128, 34], F32) for h in range(8)]
        kvx2b = [xvw(25536 + 160 * h, [128, 34], F32) for h in range(8)]
        ccol = [xvw(26816 + 32 * h, [32, 4], F32) for h in range(8)]

        ln_stats()
        wait("scalar", CV)
        op("scalar", lambda e: e.activation(out=s_t16b, in_=s_v16, func=AF.Sqrt, bias=s_cst[:, 0:1]))
        drain("scalar")
        op("scalar", lambda e: e.activation(out=s_t16c, in_=s_v16, func=AF.Sqrt,
                                            scale=s_cst[:, 1:2]), inc=CS)
        wait("vector", CS)
        op("vector", lambda e: e.reciprocal(out=s_r16, in_=s_t16b))
        drain("vector")
        op("vector", lambda e: e.tensor_tensor(out=s_t16a, in0=s_t16c, in1=s_r16, op=ALU.mult))
        drain("vector")
        op("vector", lambda e: e.tensor_scalar(out=s_t16a, in0=s_t16a, scalar1=1e-12,
                                               scalar2=None, op0=ALU.max))
        drain("vector")
        op("vector", lambda e: e.reciprocal(out=s_t16b, in_=s_t16a))
        drain("vector")
        op("vector", lambda e: e.tensor_scalar(out=s_t16b, in0=s_t16b, scalar1=32.0,
                                               scalar2=None, op0=ALU.mult))
        drain("vector")
        op("vector", lambda e: e.tensor_tensor(out=s_a16, in0=s_t16b, in1=s_r16, op=ALU.mult))
        drain("vector")
        op("vector", lambda e: e.tensor_tensor(out=s_t16a, in0=s_m16, in1=s_a16, op=ALU.mult))
        drain("vector")
        op("vector", lambda e: e.tensor_scalar(out=s_b16, in0=s_t16a, scalar1=-1.0,
                                               scalar2=None, op0=ALU.mult), inc=CV)
        cv_nrm = CV.n
        dma("sync", CD, sb3(wqpb, 256), dr3(d_wqp))
        dma("sync", CD, sb3(wkvb, 256), dr3(d_wkv))
        dma("sync", CD, sb3(woutb, H), dr3(d_woutp))
        cd_aw = CD.n
        wait("vector", CD, cd_aw)
        for c in range(NHC):
            op("vector", lambda e, c=c: e.tensor_scalar(
                out=wqpb[:, c * 256:(c + 1) * 256], in0=wqpb[:, c * 256:(c + 1) * 256],
                scalar1=s_gam[:, c:c + 1], scalar2=None, op0=ALU.mult))
        for c in range(NHC):
            op("vector", lambda e, c=c: e.tensor_scalar(
                out=wkvb[:, c * 256:(c + 1) * 256], in0=wkvb[:, c * 256:(c + 1) * 256],
                scalar1=s_gam[:, c:c + 1], scalar2=None, op0=ALU.mult))
        op("vector", lambda e: e.memset(outp[0], 0.0))
        op("vector", lambda e: e.memset(outp[1], 0.0), inc=CV)
        cv_gw = CV.n
        pq = [palloc("pq0", [128, 512]), palloc("pq1", [128, 512])]
        pkv = [palloc("pk", [128, 512]), palloc("pv_", [128, 512])]
        wait("scalar", CV, cv_nrm)
        ev_prev = None
        s2cv = {}
        for tq in range(4):
            if ev_prev is not None:
                wait("tensor", CS, ev_prev[0])
                wait("tensor", CV, ev_prev[1])
            ct_tpj = []
            for j in range(4):
                tc = tq * 4 + j
                pp = j % 2
                if j >= 2:
                    wait("scalar", CT, ct_tpj[j - 2])
                op("scalar", lambda e, tc=tc, pp=pp: e.activation(
                    out=s_y[:, pp * H:(pp + 1) * H], in_=s_x[:, tc * H:(tc + 1) * H],
                    func=AF.Identity, scale=s_a16[:, tc:tc + 1], bias=s_b16[:, tc:tc + 1]), inc=CS)
                wait("tensor", CS)
                for g in range(2):
                    bank = ptA if g == 0 else ptB
                    if g in s2cv:
                        wait("tensor", CV, s2cv[g])
                    for jj in range(4):
                        k = g * 4 + jj
                        op("tensor", lambda e, k=k, pp=pp, bank=bank, jj=jj: e.matmul(
                            bank[:, jj * 128:jj * 128 + 128],
                            s_y[:, pp * H + k * 128:pp * H + k * 128 + 128],
                            s_idf, start=True, stop=True), inc=(CT if jj == 3 else None))
                    wait("vector", CT)
                    for jj in range(4):
                        k = g * 4 + jj
                        op("vector", lambda e, k=k, j=j, bank=bank, jj=jj: e.tensor_copy(
                            out=normT[:, k * 512 + j * 128:k * 512 + j * 128 + 128],
                            in_=bank[:, jj * 128:jj * 128 + 128]), inc=(CV if jj == 3 else None))
                    s2cv[g] = CV.n
                ct_tpj.append(CT.n)
            cv_nq = CV.n
            wait("tensor", CV, cv_nq)
            if tq == 0:
                wait("tensor", CV, cv_gw)
            for k in range(NHC):
                for w in range(2):
                    op("tensor", lambda e, k=k, w=w: e.matmul(
                        pq[w], wqpb[:, k * 256 + w * 128:k * 256 + w * 128 + 128],
                        normT[:, k * 512:k * 512 + 512],
                        start=(k == 0), stop=(k == NHC - 1)),
                       inc=(CT if (k == NHC - 1 and w == 1) else None))
                for w in range(2):
                    op("tensor", lambda e, k=k, w=w: e.matmul(
                        pkv[w], wkvb[:, k * 256 + w * 128:k * 256 + w * 128 + 128],
                        normT[:, k * 512:k * 512 + 512],
                        start=(k == 0), stop=(k == NHC - 1)),
                       inc=(CT if (k == NHC - 1 and w == 1) else None))
            wait("scalar", CT)
            op("scalar", lambda e, tq=tq: e.activation(
                out=qTp[0][:, tq * 512:tq * 512 + 512], in_=pq[0], func=AF.Identity,
                scale=s_cst[:, 2:3]))
            drain("scalar")
            op("scalar", lambda e, tq=tq: e.activation(
                out=qTp[1][:, tq * 512:tq * 512 + 512], in_=pq[1], func=AF.Identity,
                scale=s_cst[:, 2:3]), inc=CS)
            cs_qe = CS.n
            wait("vector", CT)
            for kv_i, dst in ((0, k_tok), (1, v_tok)):
                op("vector", lambda e, kv_i=kv_i: e.tensor_copy(out=kvstg, in_=pkv[kv_i]),
                   inc=CV)
                wait("tensor", CV)
                bank = ptA if kv_i == 0 else ptB
                for j in range(4):
                    op("tensor", lambda e, j=j, bank=bank: e.matmul(
                        bank[:, j * 128:j * 128 + 128], kvstg[:, j * 128:j * 128 + 128],
                        s_idf, start=True, stop=True), inc=(CT if j == 3 else None))
                wait("vector", CT)
                for j in range(4):
                    tc = tq * 4 + j
                    op("vector", lambda e, tc=tc, j=j, bank=bank, dst=dst: e.tensor_copy(
                        out=dst[:, tc * 128:tc * 128 + 128],
                        in_=bank[:, j * 128:j * 128 + 128]), inc=(CV if j == 3 else None))
                wait("tensor", CV)
            ev_prev = (cs_qe, CV.n)
        pfree("pv_", "pk", "pq1", "pq0")
        pkvp = palloc("pkvp", [32, 512])
        wait("vector", CS)
        for bz in range(4):
            op("vector", lambda e, bz=bz: e.memset(fkb[bz][:, 0:1], 1.0))
            op("vector", lambda e, bz=bz: e.memset(vpb[bz][:, 16:17], 1.0),
               inc=(CV if bz == 3 else None))
        wait("scalar", CV)
        ct_kvp = []
        for h in range(8):
            for tc in range(NTC):
                gtc = h * NTC + tc
                b = gtc % 4
                if gtc >= 4:
                    wait("scalar", CT, ct_kvp[gtc - 4])
                    wait("vector", CT, ct_kvp[gtc - 4])
                op("scalar", lambda e, h=h, tc=tc, b=b: e.activation(
                    out=fkb[b][:, 1:17],
                    in_=k_tok[:, tc * 128 + 16 * h:tc * 128 + 16 * h + 16],
                    func=AF.Identity))
                op("scalar", lambda e, h=h, tc=tc, b=b: e.activation(
                    out=vpb[b][:, 0:16],
                    in_=v_tok[:, tc * 128 + 16 * h:tc * 128 + 16 * h + 16],
                    func=AF.Identity), inc=CS)
                ktile = k_tok[:, tc * 128 + 16 * h:tc * 128 + 16 * h + 16]
                in0 = ktile.unsqueeze(2).broadcast_to([128, 16, 16])
                in1 = ktile.unsqueeze(1).broadcast_to([128, 16, 16])
                o3 = fkb[b][:, 17:273].rearrange("p (a bb) -> p a bb", a=16)
                op("vector", lambda e, in0=in0, in1=in1, o3=o3: e.tensor_tensor(
                    out=o3, in0=in0, in1=in1, op=ALU.mult), inc=CV)
                wait("tensor", CV)
                wait("tensor", CS)
                op("tensor", lambda e, tc=tc, b=b: e.matmul(
                    pkvp[0:17, 0:273], vpb[b][:, 0:17], fkb[b][:, 0:273],
                    start=(tc == 0), stop=(tc == NTC - 1)), inc=CT)
                ct_kvp.append(CT.n)
            wait("scalar", CT)
            op("scalar", lambda e: e.activation(out=kvstg[0:17, 0:17],
                                                in_=pkvp[0:17, 0:17], func=AF.Identity))
            op("scalar", lambda e: e.activation(out=kvstg[0:17, 17:273],
                                                in_=pkvp[0:17, 17:273], func=AF.Identity,
                                                scale=s_cst[0:17, 3:4]), inc=CS)
            wait("tensor", CS)
            op("tensor", lambda e: e.matmul(ptA[0:16, 0:17], kvstg[0:17, 1:17],
                                            s_idf[0:17, 0:17], start=True, stop=True))
            op("tensor", lambda e: e.matmul(ptA[0:128, 32:49], kvstg[0:17, 17:145],
                                            s_idf[0:17, 0:17], start=True, stop=True))
            op("tensor", lambda e: e.matmul(ptA[0:128, 64:81], kvstg[0:17, 145:273],
                                            s_idf[0:17, 0:17], start=True, stop=True), inc=CT)
            if h == 0:
                for hz in range(8):
                    op("vector", lambda e, hz=hz: e.memset(kvk[hz], 0.0))
                    op("vector", lambda e, hz=hz: e.memset(kvx2a[hz], 0.0))
                    op("vector", lambda e, hz=hz: e.memset(kvx2b[hz], 0.0))
                    op("vector", lambda e, hz=hz: e.memset(ccol[hz], 0.0),
                       inc=(CV if hz == 7 else None))
                wait("scalar", CV)
            wait("scalar", CT)
            op("scalar", lambda e, h=h: e.activation(out=kvk[h][0:16, 0:16],
                                                     in_=ptA[0:16, 0:16], func=AF.Identity))
            op("scalar", lambda e, h=h: e.activation(out=kvk[h][0:16, 32:33],
                                                     in_=ptA[0:16, 16:17], func=AF.Identity))
            op("scalar", lambda e, h=h: e.activation(out=kvx2a[h][:, 0:16],
                                                     in_=ptA[0:128, 32:48], func=AF.Identity))
            op("scalar", lambda e, h=h: e.activation(out=kvx2a[h][:, 32:33],
                                                     in_=ptA[0:128, 48:49], func=AF.Identity))
            op("scalar", lambda e, h=h: e.activation(out=kvx2b[h][:, 0:16],
                                                     in_=ptA[0:128, 64:80], func=AF.Identity))
            op("scalar", lambda e, h=h: e.activation(out=kvx2b[h][:, 32:33],
                                                     in_=ptA[0:128, 80:81], func=AF.Identity))
            op("scalar", lambda e, h=h: e.activation(out=ccol[h][0:17, 0:1],
                                                     in_=kvstg[0:17, 0:1], func=AF.Identity),
               inc=CS)
            wait("tensor", CS)
        pfree("pkvp")
        ptC = palloc("ptC", [128, 512])
        ptD = palloc("ptD", [128, 512])
        po16 = [palloc("po0", [33, 512]), palloc("po1", [33, 512])]
        ob = []
        ct_den = []
        nb = 0
        ob_v = []
        ctfq_last = []
        cs_evac = []
        for tq in range(4):
            wait("tensor", CS)
            qcv = {}
            for j in range(4):
                tc = tq * 4 + j
                bpq = [(ptA, ptB), (ptC, ptD)][j % 2]
                if j % 2 in qcv:
                    wait("tensor", CS, qcv[j % 2])
                for w in range(2):
                    bank = bpq[w]
                    op("tensor", lambda e, tc=tc, w=w, bank=bank: e.matmul(
                        bank[:, 0:128],
                        qTp[w][:, tc * 128:tc * 128 + 128],
                        s_idf, start=True, stop=True), inc=(CT if w == 1 else None))
                wait("scalar", CT)
                for w in range(2):
                    bank = bpq[w]
                    op("scalar", lambda e, j=j, w=w, bank=bank: e.activation(
                        out=qtq[:, j * 256 + w * 128:j * 256 + w * 128 + 128],
                        in_=bank[:, 0:128],
                        func=AF.Identity), inc=(CS if w == 1 else None))
                qcv[j % 2] = CS.n
            cs_qtq = CS.n
            wait("vector", CS, cs_qtq)
            for h in range(8):
                ff = nb % 2
                fq2T = fq2Tb[ff]
                qh = qhb[ff]
                if nb >= 1:
                    wait("vector", CT, ctfq_last[nb - 1])
                if nb >= 2:
                    wait("scalar", CT, ct_den[nb - 2])
                ct_fq = []
                for j in range(4):
                    b = j % 2
                    qsl = qtq[:, j * 256 + 32 * h:j * 256 + 32 * h + 16]
                    in0 = qsl.unsqueeze(2).broadcast_to([128, 16, 16])
                    in1 = qsl.unsqueeze(1).broadcast_to([128, 16, 16])
                    if j >= 2:
                        wait("vector", CT, ct_fq[j - 2])
                    o3 = fq2[b].rearrange("p (a bb) -> p a bb", a=16)
                    op("vector", lambda e, in0=in0, in1=in1, o3=o3: e.tensor_tensor(
                        out=o3, in0=in0, in1=in1, op=ALU.mult), inc=CV)
                    wait("tensor", CV)
                    if len(cs_evac) >= 2:
                        wait("tensor", CS, cs_evac[-2])
                    bpair = [(ptA, ptB), (ptC, ptD)][j % 2]
                    for w in range(2):
                        bank = bpair[w]
                        op("tensor", lambda e, b=b, w=w, bank=bank: e.matmul(
                            bank[:, 0:128],
                            fq2[b][:, w * 128:w * 128 + 128],
                            s_ids, start=True, stop=True), inc=(CT if w == 1 else None))
                    ct_fq.append(CT.n)
                    wait("scalar", CT)
                    for w in range(2):
                        bank = bpair[w]
                        op("scalar", lambda e, j=j, w=w, bank=bank, fq2T=fq2T:
                           e.activation(
                               out=fq2T[:, w * 512 + j * 128:w * 512 + j * 128 + 128],
                               in_=bank[:, 0:128],
                               func=AF.Identity), inc=(CS if w == 1 else None))
                    cs_evac.append(CS.n)
                ctfq_last.append(ct_fq[-1])
                op("scalar", lambda e, h=h, tq=tq, qh=qh: e.activation(
                    out=qh[0:32, 0:512],
                    in_=qTp[h // 4][32 * (h % 4):32 * (h % 4) + 32,
                                    tq * 512:tq * 512 + 512], func=AF.Identity), inc=CS)
                cs_fq = CS.n
                bo = po16[h % 2]
                if nb >= 2:
                    wait("tensor", CS, ob[nb - 2])
                    wait("tensor", CV, ob_v[nb - 2])
                wait("tensor", CS, cs_fq)
                op("tensor", lambda e, h=h, bo=bo, qh=qh: e.matmul(
                    bo[0:33, :], kvk[h][:, 0:33], qh[0:32, 0:512],
                    start=True, stop=False))
                op("tensor", lambda e, h=h, bo=bo, fq2T=fq2T: e.matmul(
                    bo[0:33, :], kvx2a[h][:, 0:33], fq2T[:, 0:512],
                    start=False, stop=False))
                op("tensor", lambda e, h=h, bo=bo, fq2T=fq2T: e.matmul(
                    bo[0:33, :], kvx2b[h][:, 0:33], fq2T[:, 512:1024],
                    start=False, stop=True), inc=CT)
                ct_den.append(CT.n)
                wait("scalar", CT)
                op("scalar", lambda e, h=h, tq=tq, bo=bo: e.activation(
                    out=outp[h // 4][32 * (h % 4):32 * (h % 4) + 16,
                                     tq * 512:tq * 512 + 512],
                    in_=bo[0:16, :], func=AF.Identity,
                    bias=ccol[h][0:16, 0:1]), inc=CS)
                ob.append(CS.n)
                wait("vector", CT)
                dbuf, dbase = den_slot(h)
                op("vector", lambda e, bo=bo, dbuf=dbuf, dbase=dbase, tq=tq:
                   e.tensor_scalar(
                       out=dbuf[dbase:dbase + 1, tq * 512:tq * 512 + 512],
                       in0=bo[32:33, 0:512], scalar1=2048.0, scalar2=1e-5,
                       op0=ALU.add, op1=ALU.max), inc=CV)
                ob_v.append(CV.n)
                nb += 1
        cs_outp = CS.n
        pfree("po1", "po0", "ptD", "ptC")
        dgath = pvw(73728, [128, 2048], F32)
        drain("vector")
        wait("vector", CS, cs_outp)
        wait("sync", CV, ob_v[-1])
        for h in range(8):
            dbuf, dbase = den_slot(h)
            dma("sync", CD, dgath[h:h + 1, :], dbuf[dbase:dbase + 1, :])
        wait("vector", CD)
        op("vector", lambda e: e.reciprocal(out=dgath[0:8, :], in_=dgath[0:8, :]),
           inc=CV)
        wait("sync", CV)
        for h in range(8):
            dbuf, dbase = den_slot(h)
            dma("sync", CD, dbuf[dbase:dbase + 1, :], dgath[h:h + 1, :])
        wait("tensor", CD)
        pbc = palloc("pbc", [128, 2048])
        po = [palloc("poA", [128, 512]), palloc("poB", [128, 512])]
        band_tt = {}
        for half in range(2):
            for hh in range(4):
                h = half * 4 + hh
                dbuf, dbase = den_slot(h)
                band = 32 * hh if hh < 3 else 0
                if band in band_tt:
                    wait("tensor", CV, band_tt[band])
                for seg in range(4):
                    op("tensor", lambda e, dbuf=dbuf, dbase=dbase, band=band, seg=seg:
                       e.matmul(
                           pbc[band:band + 32, seg * 512:seg * 512 + 512],
                           s_ones[dbase:dbase + 1, 0:32],
                           dbuf[dbase:dbase + 1, seg * 512:seg * 512 + 512],
                           start=True, stop=True), inc=(CT if seg == 3 else None))
                wait("vector", CT)
                op("vector", lambda e, half=half, hh=hh, band=band: e.tensor_tensor(
                    out=outs[half][32 * hh:32 * hh + 32, :],
                    in0=outp[half][32 * hh:32 * hh + 32, :],
                    in1=pbc[band:band + 32, :], op=ALU.mult), inc=CV)
                band_tt[band] = CV.n
        cv_outs = CV.n
        wait("tensor", CV, cv_outs)
        deo = []
        for tc in range(NTC):
            if tc >= 1:
                wait("tensor", CV, deo[tc - 1])
            for c in range(2):
                for hf in range(2):
                    op("tensor", lambda e, tc=tc, c=c, hf=hf: e.matmul(
                        po[hf], outs[c][:, tc * 128:tc * 128 + 128],
                        woutb[:, c * 1024 + hf * 512:c * 1024 + hf * 512 + 512],
                        start=(c == 0), stop=(c == 1)),
                       inc=(CT if (c == 1 and hf == 1) else None))
            wait("vector", CT)
            for hf in range(2):
                op("vector", lambda e, tc=tc, hf=hf: e.tensor_tensor(
                    out=s_x[:, tc * H + hf * 512:tc * H + hf * 512 + 512],
                    in0=s_x[:, tc * H + hf * 512:tc * H + hf * 512 + 512],
                    in1=po[hf], op=ALU.add), inc=(CV if hf == 1 else None))
            deo.append(CV.n)
            drain("vector")
        pfree("poB", "poA", "pbc")
        return dict(outp=outp, outs=outs, qTp=qTp, kvk=kvk, kvx2a=kvx2a, kvx2b=kvx2b,
                    k_tok=k_tok, v_tok=v_tok, qtq=qtq, ccol=ccol,
                    sden=[sdenAB[0], sdenAB[1], sdenC])

    if "1" in PH:
        ffn_pre(0)
        qw_state["cs0"] = CS.n
        cws_gu, cws_all = quant_all()
        ffn(0, pre_done=True)
        if "ff" in DBG:
            dcols = dump_decl("cols", [128, 168])
            dsc = dump_decl("sc", [1, 16])
            dcol8 = dump_decl("col8", [128, 8])
            dxi = dump_decl("xi", [128, 2048])
            dsg0 = dump_decl("sg0", [128, 2048])
            barrier(("sync", "vector"))
            packs = [(s_m16, 16), (s_v16, 16), (s_r16, 16), (s_a16, 16), (s_b16, 16),
                     (s_ymax, 16), (s_deqg, 24), (s_sgmax, 24), (s_swg, 24)]
            dpk = pvw(0, [128, 168], F32)
            off = 0
            for tsrc, wdt in packs:
                op("vector", lambda e, t=tsrc, o=off, w=wdt: e.tensor_copy(
                    out=dpk[:, o:o + w], in_=t), inc=None)
                off += wdt
            op("vector", lambda e: e.tensor_copy(out=s_y[:, 0:2048],
                                                 in_=s_xiT[:, 0:2048]), inc=CV)
            wait("sync", CV)
            dma("sync", CD, dcols, dpk)
            dma("sync", CD, dsc, s_sc)
            dma("sync", CD, dcol8, s_col)
            dma("sync", CD, dxi, s_y[:, 0:2048])
            wait("sync", CD)
            dma("sync", CD, dsg0, d_sg[0:128, :])
            dswdb2 = dump_decl("swdb2", [128, 1024])
            wait("sync", CV)
            dma("sync", CD, dswdb2, s_swdb2)
        if "x1" in DBG:
            dx1 = dump_decl("x1", [T, H])
            barrier(("sync",))
            for tc in range(NTC):
                dma("sync", CD, dx1[128 * tc:128 * (tc + 1), :], s_x[:, tc * H:(tc + 1) * H])
    if "1" not in PH:
        cws_gu, cws_all = quant_all()
        barrier()
    if "2" in PH:
        barrier()
        att_dbg = attention()
        if "at" in DBG:
            drden = dump_decl("rden", [32, 2048])
            doutp0 = dump_decl("outp0", [128, 2048])
            douts0 = dump_decl("outs0", [128, 2048])
            dqtp0 = dump_decl("qtp0", [128, 2048])
            barrier(("sync", "vector"))
            for h in range(8):
                if h % 4 < 3:
                    src = att_dbg["sden"][h // 4][32 * (h % 4):32 * (h % 4) + 1, :]
                else:
                    src = att_dbg["sden"][2][32 * (h // 4):32 * (h // 4) + 1, :]
                dma("sync", CD, drden[h:h + 1, :], src)
            dma("sync", CD, doutp0, att_dbg["outp"][0])
            dma("sync", CD, douts0, att_dbg["outs"][0])
            dma("sync", CD, dqtp0, att_dbg["qTp"][0])
        if "kv" in DBG:
            dktok = dump_decl("ktok", [128, 2048])
            dvtok = dump_decl("vtok", [128, 2048])
            dkvst = dump_decl("kvst", [128, 64])
            barrier(("sync", "vector"))
            kpk = att_dbg["qtq"]
            op("vector", lambda e: e.memset(kpk[:, 0:64], 0.0))
            drain("vector")
            op("vector", lambda e: e.tensor_copy(out=kpk[0:32, 0:16],
                                                 in_=att_dbg["kvk"][0][:, 0:16]))
            op("vector", lambda e: e.tensor_copy(out=kpk[0:32, 16:17],
                                                 in_=att_dbg["kvk"][0][:, 32:33]))
            op("vector", lambda e: e.tensor_copy(out=kpk[:, 18:34],
                                                 in_=att_dbg["kvx2a"][0][:, 0:16]))
            op("vector", lambda e: e.tensor_copy(out=kpk[:, 34:35],
                                                 in_=att_dbg["kvx2a"][0][:, 32:33]))
            op("vector", lambda e: e.tensor_copy(out=kpk[:, 36:52],
                                                 in_=att_dbg["kvx2b"][0][:, 0:16]))
            op("vector", lambda e: e.tensor_copy(out=kpk[:, 52:53],
                                                 in_=att_dbg["kvx2b"][0][:, 32:33]))
            op("vector", lambda e: e.tensor_copy(out=kpk[0:32, 54:58],
                                                 in_=att_dbg["ccol"][0]), inc=CV)
            stg = att_dbg["qTp"][0]
            op("vector", lambda e: e.tensor_copy(out=stg, in_=att_dbg["k_tok"]), inc=CV)
            wait("sync", CV)
            dma("sync", CD, dktok, stg)
            wait("vector", CD)
            op("vector", lambda e: e.tensor_copy(out=stg, in_=att_dbg["v_tok"]), inc=CV)
            wait("sync", CV)
            dma("sync", CD, dvtok, stg)
            dma("sync", CD, dkvst, kpk[:, 0:64])
        if "x2" in DBG:
            dx2 = dump_decl("x2", [T, H])
            barrier(("sync",))
            for tc in range(NTC):
                dma("sync", CD, dx2[128 * tc:128 * (tc + 1), :], s_x[:, tc * H:(tc + 1) * H])
    if "3" in PH:
        wait("tensor", CS)
        wait("tensor", CV)
        wait("sync", CT)
        wait("sync", CV)
        ffn(1)
    else:
        barrier(("sync",))
        for tc in range(NTC):
            dma("sync", CD, d_out[128 * tc:128 * (tc + 1), :], s_x[:, tc * H:(tc + 1) * H])
    barrier(("sync",))
    wait("sync", CD)

    with contextlib.ExitStack() as stack:
        for c in sems:
            c.h = stack.enter_context(nc.semaphore(c.name))
        with nc.Block() as block:
            @block.sync
            def _(e):
                for f in q["sync"]:
                    f(e)

            @block.scalar
            def _(e):
                for f in q["scalar"]:
                    f(e)

            @block.vector
            def _(e):
                for f in q["vector"]:
                    f(e)

            @block.tensor
            def _(e):
                for f in q["tensor"]:
                    f(e)

            @block.gpsimd
            def _(e):
                for f in q["gpsimd"]:
                    f(e)
    return nc, dumps


_CACHE = {}


def kernel(L0_attn_x, L0_attn_gamma, L0_attn_w_q, L0_attn_w_kv, L0_attn_w_out,
           L0_ffn_clip_x, L0_ffn_clip_down, L0_ffn_w_gate, L0_ffn_w_up, L0_ffn_w_down):
    if "nc" not in _CACHE:
        _CACHE["nc"], _CACHE["dumps"] = build_nc()
    nc, dumps = _CACHE["nc"], _CACHE["dumps"]
    x = np.ascontiguousarray(np.asarray(L0_attn_x, dtype=np.float32))
    wq = np.asarray(L0_attn_w_q, np.float32)
    wqp = np.zeros((H, 256), np.float32)
    for h in range(8):
        wqp[:, 32 * h:32 * h + 16] = wq[:, 16 * h:16 * h + 16]
    wo = np.asarray(L0_attn_w_out, np.float32)
    wop = np.zeros((256, H), np.float32)
    for h in range(8):
        wop[32 * h:32 * h + 16, :] = wo[16 * h:16 * h + 16, :]
    common = dict(
        gamma=np.ascontiguousarray(np.asarray(L0_attn_gamma, np.float32)),
        wqp=wqp, wkv=np.ascontiguousarray(np.asarray(L0_attn_w_kv, np.float32)),
        woutp=wop,
        clipx=np.asarray(L0_ffn_clip_x, np.float32).reshape(1, 1),
        clipd=np.asarray(L0_ffn_clip_down, np.float32).reshape(1, 1),
        wg=np.ascontiguousarray(np.asarray(L0_ffn_w_gate, np.float32)),
        wu=np.ascontiguousarray(np.asarray(L0_ffn_w_up, np.float32)),
        wd=np.ascontiguousarray(np.asarray(L0_ffn_w_down, np.float32)),
        idf=np.eye(128, dtype=np.float32),
        ids=(np.eye(128) * np.sqrt(0.5)).astype(np.float32),
    )
    in_maps = [dict(common, x=x[c]) for c in range(8)]
    res = run_bass_kernel_spmd(nc, in_maps, core_ids=list(range(8)))
    kernel.last = res
    return np.stack([res.results[c]["out"] for c in range(8)], axis=0)
